# version 13
# baseline (speedup 1.0000x reference)
import numpy as np
import ml_dtypes
from contextlib import ExitStack

import concourse.bass as bass
import concourse.mybir as mybir
from concourse.bass_utils import run_bass_kernel_spmd

F32 = mybir.dt.float32
BF16 = mybir.dt.bfloat16
AF = mybir.ActivationFunctionType
ALU = mybir.AluOpType
BF = ml_dtypes.bfloat16

DM = 1024
DI = 2048
DS = 16
DCONV = 4
DTR = 32
NE = 8
FF = 4096
NH = 16
HD = 64
LAT = 32
NDB = DI // 128
NM = DM // 128

NEG = -1.0e30
NE_RUN = NE
DBG_GATE = False
P6_NH = NH
P6_TAIL = True



def _cols(b):
    b = np.asarray(b, np.float32)
    return np.ascontiguousarray(b.reshape(-1, 128).T)


def _db_regroup(a):
    a = np.asarray(a, np.float32)
    k = a.shape[1]
    return np.ascontiguousarray(a.reshape(NDB, 128, k).transpose(1, 0, 2).reshape(128, NDB * k))


def _rope_tables(T):
    inv_freq = 1.0 / 10000.0 ** (np.arange(0, LAT, 2, dtype=np.float64) / LAT)
    t = np.arange(T, dtype=np.float64)
    freqs = np.outer(t, inv_freq)
    cos = np.repeat(np.cos(freqs), 2, axis=-1)[:, :LAT]
    sin = np.repeat(np.sin(freqs), 2, axis=-1)[:, :LAT]
    return cos, sin


def host_prep(inputs, T):
    d = {}
    g = lambda n: np.asarray(inputs[n], np.float32)

    x = g("L0_attn_x")
    xfs = [np.ascontiguousarray(x[c].T) for c in range(x.shape[0])]

    d["wiT"] = np.ascontiguousarray(g("L0_attn_Wi").T)
    d["bi_c"] = _cols(g("L0_attn_bi"))
    d["convw_c"] = _db_regroup(g("L0_attn_conv_w")[:, 0, :])
    d["convb_c"] = _cols(g("L0_attn_conv_b"))
    d["wdtT"] = np.ascontiguousarray(g("L0_attn_Wdt").T)
    d["bdt_c"] = _cols(g("L0_attn_bdt"))
    d["alog_c"] = _db_regroup(g("L0_attn_A_log"))
    d["bm_c"] = _db_regroup(g("L0_attn_Bm"))
    d["cm_c"] = _db_regroup(g("L0_attn_Cm"))
    d["dv_c"] = _cols(g("L0_attn_Dv"))
    d["woT"] = np.ascontiguousarray(g("L0_attn_Wo").T)
    d["bo_c"] = _cols(g("L0_attn_bo"))

    d["wg"] = g("L0_ffn_Wg")
    d["bg_c"] = np.ascontiguousarray(g("L0_ffn_bg").reshape(8, 1))
    d["w1"] = g("L0_ffn_W1")
    d["b1_c"] = np.stack([_cols(b) for b in g("L0_ffn_b1")])
    d["w2"] = g("L0_ffn_W2")
    d["b2_c"] = np.stack([_cols(b) for b in g("L0_ffn_b2")])

    d["catT"] = np.ascontiguousarray(g("L1_attn_c_attn_w").T)
    d["cab_c"] = _cols(g("L1_attn_c_attn_b"))
    q2lT = np.ascontiguousarray(g("L1_attn_q2l_w").T)
    d["q2lT_r"] = np.concatenate([q2lT, q2lT], 0)
    d["q2lb_r"] = np.concatenate([g("L1_attn_q2l_b")] * 2).reshape(64, 1).astype(np.float32)
    kv2lT = np.ascontiguousarray(g("L1_attn_kv2l_w").T)
    d["kv2lT_r"] = np.concatenate([kv2lT, kv2lT], 0)
    d["kv2lb_r"] = np.concatenate([g("L1_attn_kv2l_b")] * 2).reshape(64, 1).astype(np.float32)
    wqkT = np.ascontiguousarray(g("L1_attn_wqk_w").T)
    d["wqkT_r"] = np.concatenate([wqkT, wqkT], 0)
    d["wqkb_r"] = np.concatenate([g("L1_attn_wqk_b")] * 2).reshape(64, 1).astype(np.float32)
    l2o = g("L1_attn_l2o_w")
    d["l2o_r"] = np.concatenate([l2o, l2o], 0)
    d["l2ob_t"] = _cols(np.tile(g("L1_attn_l2o_b"), NH))
    d["cprojT"] = np.ascontiguousarray(g("L1_attn_cproj_w").T)
    d["cpb_c"] = _cols(g("L1_attn_cproj_b"))

    d["l1w0"] = g("L1_ffn_weight")
    d["l1w1"] = g("L1_ffn_aux_weight1")
    d["l1w2"] = g("L1_ffn_aux_weight2")
    d["l1b_c"] = _cols(g("L1_ffn_bias"))

    d["ident"] = np.eye(128, dtype=BF)
    d["identf"] = np.eye(128, dtype=np.float32)
    d["ones_col"] = np.ones((128, 1), BF)
    d["ones_row"] = np.ones((1, 512), BF)
    d["ones_colf"] = np.ones((128, 1), np.float32)
    d["ones_rowf"] = np.ones((1, 128), np.float32)
    cos, sin = _rope_tables(T)
    d["cos_r"] = np.concatenate([cos.T, cos.T], 0).astype(BF)
    d["sin_r"] = np.concatenate([sin.T, sin.T], 0).astype(BF)
    R = np.zeros((LAT, LAT), np.float32)
    for i in range(LAT // 2):
        R[2 * i, 2 * i + 1] = -1.0
        R[2 * i + 1, 2 * i] = 1.0
    RT = np.ascontiguousarray(R.T)
    d["rT_r"] = np.concatenate([RT, RT], 0).astype(BF)
    TCW = min(512, T)
    NR = TCW // 128
    m = np.zeros((128, NR * TCW), BF)
    p = np.arange(128).reshape(128, 1)
    f = np.arange(TCW).reshape(1, TCW)
    for r in range(NR):
        m[:, r * TCW:(r + 1) * TCW] = (f >= 128 * r + p).astype(BF)
    d["mask01"] = m
    return d, xfs



def park_specs(T):
    NKC = T // 128
    return {
        "xinb": ([DI, T], BF16),
        "x32b": ([DTR, T], BF16),
        "szb": ([DI, T], BF16),
        "yzb": ([DI, T], BF16),
        "x1f": ([DM, T], F32),
        "x1b": ([DM, T], BF16),
        "wrow": ([NE, T], F32),
        "facc": ([DM, T], F32),
        "x2f": ([DM, T], F32),
        "x2b": ([DM, T], BF16),
        "qfkp": ([4 * 128, T], BF16),
        "krotp": ([4 * 128, T], BF16),
        "vaugp": ([128, NH * 33 * NKC], BF16),
        "cphp": ([4 * 128, DM], BF16),
        "cbias": ([128, NM], F32),
        "dvm1": ([128, 1], F32),
        "x3f": ([DM, T], F32),
        "x3b": ([DM, T], BF16),
        "a1p": ([DM, T], F32),
        "f1p": ([DM, T], F32),
        "qkvp": ([3 * DM, T], BF16),
    }


PARK_PROD = {
    "xinb": "p1", "x32b": "p1", "szb": "p1",
    "yzb": "p2",
    "x1f": "p3", "x1b": "p3",
    "wrow": "p4", "facc": "p4",
    "x2f": "p4", "x2b": "p4",
    "qfkp": "p5", "krotp": "p5", "vaugp": "p5",
    "cphp": "p0", "cbias": "p0", "dvm1": "p0",
    "x3f": "p6", "x3b": "p6",
    "a1p": "p6", "f1p": "p7",
    "qkvp": "p5",
}
PARK_CONS = {
    "xinb": ["p2"], "x32b": ["p2"], "szb": ["p2"],
    "yzb": ["p3"],
    "x1f": ["p4"], "x1b": ["p4"],
    "wrow": ["p4", "dbg"], "facc": ["p4", "dbg"],
    "x2f": ["p6"], "x2b": ["p5"],
    "qfkp": ["p6"], "krotp": ["p6"], "vaugp": ["p6"],
    "cphp": ["p6"], "cbias": ["p6"], "dvm1": ["p3"],
    "x3f": ["p7"], "x3b": ["p7"],
    "a1p": ["p6"], "f1p": ["p7"],
    "qkvp": ["p5", "dbg"],
}

PHASE_ORDER = ["p0", "p1", "p2", "p3", "p4", "p5", "p6", "p7"]

PHASE_INPUTS = {
    "p0": ["dv_c", "l2o_r", "cprojT", "cpb_c", "l2ob_t", "ident", "ones_col"],
    "p1": ["xf", "wiT", "bi_c"],
    "p2": ["wdtT", "bdt_c", "convw_c", "convb_c", "alog_c", "bm_c", "cm_c", "x32b_in" if False else "wdtT"],
    "p3": ["xf", "woT", "bo_c", "ones_row"],
    "p4": ["wg", "bg_c", "w1", "b1_c", "w2", "b2_c", "ident", "ones_col", "ones_row"],
    "p5": ["catT", "cab_c", "q2lT_r", "q2lb_r", "kv2lT_r", "kv2lb_r", "wqkT_r", "wqkb_r",
           "cos_r", "sin_r", "rT_r", "ident"],
    "p6": ["mask01", "ones_col", "ones_row"],
    "p7": ["l1w0", "l1w1", "l1w2", "l1b_c"],
}

INPUT_SHAPES = None



class Ctx:
    def __init__(self, nc, T, stack, phases, prep_shapes):
        self.nc = nc
        self.T = T
        self.TC = min(512, T)
        self.NT = T // self.TC
        self.stack = stack
        self.phases = phases
        self.prep_shapes = prep_shapes
        self.sems = {}
        self.phase_semlist = []
        self.inputs = {}
        self.parks = {}
        self.out = None

    def sem(self, name):
        if name not in self.sems:
            h = self.nc.alloc_semaphore(name=name)
            self.sems[name] = [h, 0]
            if name != "bar" and "_q" not in name:
                self.phase_semlist.append(h)
        return self.sems[name][0]

    def bump(self, name, k):
        self.sem(name)
        self.sems[name][1] += k
        return self.sems[name][1]

    def val(self, name):
        self.sem(name)
        return self.sems[name][1]

    def inp(self, name):
        if name not in self.inputs:
            shape, npdt = self.prep_shapes[name]
            dt = BF16 if npdt == BF else F32
            self.inputs[name] = self.nc.dram_tensor("i_" + name, list(shape), dt,
                                                    kind="ExternalInput").ap()
        return self.inputs[name]

    def park(self, name):
        if name not in self.parks:
            shape, dt = park_specs(self.T)[name]
            prod = PARK_PROD[name]
            cons = PARK_CONS[name]
            in_here = prod in self.phases
            used_here = any(c in self.phases for c in cons)
            used_later = any(c not in self.phases for c in cons)
            if in_here and not used_later:
                kind = "Internal"
            elif in_here:
                kind = "ExternalOutput"
            else:
                kind = "ExternalInput"
            self.parks[name] = (self.nc.dram_tensor("k_" + name, shape, dt, kind=kind).ap(), kind)
        return self.parks[name][0]

    def begin_phase(self):
        self.pstack = ExitStack()

    def end_phase(self):
        self.pstack.close()

    def out_park(self):
        if self.out is None:
            self.out = self.nc.dram_tensor("outf", [DM, self.T], F32,
                                           kind="ExternalOutput").ap()
        return self.out

    def sbuf(self, name, shape, dt):
        return self.pstack.enter_context(self.nc.sbuf_tensor(name, shape, dt))

    def psum(self, name, shape, dt=F32):
        return self.pstack.enter_context(self.nc.psum_tensor(name, shape, dt))


def barrier(ctx, block, tag):
    b = ctx.sem("bar")
    for engf in (block.sync, block.vector, block.scalar, block.tensor, block.gpsimd):
        def mk(engf):
            def f(eng):
                eng.sem_inc(b, 1)
            return f
        engf(mk(engf))
    tgt = ctx.bump("bar", 5)
    for engf in (block.sync, block.vector, block.scalar, block.tensor, block.gpsimd):
        def mk2(engf):
            def f(eng):
                eng.wait_ge(b, tgt)
            return f
        engf(mk2(engf))



def emit_p0(ctx, block):
    dv = ctx.inp("dv_c"); l2o = ctx.inp("l2o_r"); cpT = ctx.inp("cprojT")
    cpb = ctx.inp("cpb_c"); l2ob = ctx.inp("l2ob_t")
    onc = ctx.inp("ones_colf"); onr = ctx.inp("ones_rowf")
    dvm1 = ctx.park("dvm1"); cphp = ctx.park("cphp"); cbias = ctx.park("cbias")

    dv_sb = ctx.sbuf("p0_dv", [128, 16], F32)
    dvs = ctx.sbuf("p0_dvs", [128, 1], F32)
    onc_sb = ctx.sbuf("p0_onc", [128, 1], F32)
    onr_sb = ctx.sbuf("p0_onr", [1, 128], F32)
    sc_b = ctx.sbuf("p0_scb", [1, 1], F32)
    dvm_sb = ctx.sbuf("p0_dvm", [128, 1], F32)
    cpT_sb = ctx.sbuf("p0_cpT", [128, 8 * 1024], BF16)
    l2o_sb = ctx.sbuf("p0_l2o", [128, 32], BF16)
    l2ob_sb = ctx.sbuf("p0_l2ob", [128, 8], BF16)
    cpb_sb = ctx.sbuf("p0_cpb", [128, 8], F32)
    cph_sb = ctx.sbuf("p0_cph", [128, 4 * 1024], BF16)
    cbias_sb = ctx.sbuf("p0_cbias", [128, 8], F32)
    ps_a = ctx.psum("p0_psa", [128, 512], F32)
    ps_b = ctx.psum("p0_psb", [128, 512], F32)
    ps_1 = ctx.psum("p0_ps1", [1, 1], F32)
    ps_c = ctx.psum("p0_psc", [128, 8], F32)
    dmy0 = ctx.sbuf("p0_dmy", [1, 1], F32)

    qs = ctx.sem("p0_qs"); qg = ctx.sem("p0_qg")
    v0 = ctx.sem("p0_v0"); v1 = ctx.sem("p0_v1"); v2 = ctx.sem("p0_v2")
    t0 = ctx.sem("p0_t0"); t1 = ctx.sem("p0_t1"); t2 = ctx.sem("p0_t2")
    a0 = ctx.sem("p0_a0")

    @block.sync
    def _(sy):
        for ap, t in ((dv, dv_sb), (cpb, cpb_sb)):
            sy.dma_start(out=t[:, :], in_=ap).then_inc(qs, 16)

    @block.gpsimd
    def _(g):
        for ap, t in ((onc, onc_sb), (onr, onr_sb), (l2o, l2o_sb), (l2ob, l2ob_sb)):
            g.dma_start(out=t[:, :], in_=ap).then_inc(qg, 16)
        for k in range(8):
            g.dma_start(out=cpT_sb[:, k * 1024:(k + 1) * 1024],
                        in_=cpT[k * 128:(k + 1) * 128, :]).then_inc(qg, 16)

    @block.vector
    def _(v):
        v.wait_ge(qs, 32)
        v.reduce_sum(dvs[:, :], dv_sb[:, :], axis=mybir.AxisListType.X)
        v.tensor_copy(dvm_sb[:, :], dvs[:, :]).then_inc(v0, 1)

    @block.scalar
    def _(s):
        s.memzero(dmy0[:, :])
        s.wait_ge(t0, 1)
        s.activation(dmy0[:, :], dmy0[:, :], AF.Identity)
        s.activation(dmy0[:, :], dmy0[:, :], AF.Identity)
        s.activation(sc_b[:, :], ps_1[:, :], AF.Copy, scale=1.0 / DI, bias=1.0)
        s.activation(dmy0[:, :], sc_b[:, :], AF.Identity).then_inc(a0, 1)

    @block.tensor
    def _(t):
        t.wait_ge(qg, 16 * 12)
        t.wait_ge(v0, 1)
        t.matmul(ps_a[0:32, :], l2o_sb[0:64, :], cpT_sb[0:64, 0:512], start=True, stop=True)
        t.matmul(ps_1[:, :], dvs[:, :], onc_sb[:, :], start=True, stop=True).then_inc(t0, 1)
        t.wait_ge(a0, 1)
        t.matmul(ps_a[0:32, :], l2o_sb[0:64, :], cpT_sb[0:64, 0:512], start=True, stop=True)
        t.matmul(ps_c[:, 0:1], onr_sb[:, :], sc_b[:, :], start=True, stop=True).then_inc(t0, 1)
        for h in range(NH):
            base = 64 * (h % 2)
            for half in range(2):
                i = 2 * h + half
                ps = ps_a if i % 2 == 0 else ps_b
                if i >= 2:
                    t.wait_ge(v1, i - 1)
                t.matmul(ps[0:32, :], l2o_sb[base:base + 64, :],
                         cpT_sb[base:base + 64, (h // 2) * 1024 + half * 512:(h // 2) * 1024 + half * 512 + 512],
                         start=True, stop=True).then_inc(t1, 1)
        t.wait_ge(v2, 1)
        for m in range(8):
            for k in range(8):
                ins = t.matmul(ps_c[:, m:m + 1],
                               cpT_sb[:, k * 1024 + 128 * m:k * 1024 + 128 * m + 128],
                               l2ob_sb[:, k:k + 1], start=(k == 0), stop=(k == 7))
        ins.then_inc(t2, 1)

    @block.vector
    def _(v):
        v.wait_ge(t0, 2)
        v.tensor_copy(dvm_sb[:, :], ps_c[:, 0:1]).then_inc(v2, 1)
        for h in range(NH):
            gI = h // 4; b = h % 4
            for half in range(2):
                i = 2 * h + half
                v.wait_ge(t1, i + 1)
                ps = ps_a if i % 2 == 0 else ps_b
                v.tensor_copy(cph_sb[32 * b:32 * b + 32, gI * 1024 + half * 512:gI * 1024 + half * 512 + 512],
                              ps[0:32, :]).then_inc(v1, 1)
        v.wait_ge(t2, 1)
        v.tensor_add(cbias_sb[:, :], cpb_sb[:, :], ps_c[:, :]).then_inc(v2, 1)

    @block.sync
    def _(sy):
        sy.wait_ge(v2, 2)
        sy.wait_ge(v1, 32)
        sy.dma_start(out=dvm1, in_=dvm_sb[:, :]).then_inc(qs, 16)
        sy.dma_start(out=cbias, in_=cbias_sb[:, :]).then_inc(qs, 16)
        for gI in range(4):
            sy.dma_start(out=cphp[gI * 128:(gI + 1) * 128, :],
                         in_=cph_sb[:, gI * 1024:(gI + 1) * 1024]).then_inc(qs, 16)
        sy.wait_ge(qs, 16 * 8)



def emit_p1(ctx, block):
    T, TC, NT = ctx.T, ctx.TC, ctx.NT
    xf = ctx.inp("xf"); wiT = ctx.inp("wiT"); bi = ctx.inp("bi_c")
    xinb = ctx.park("xinb"); x32b = ctx.park("x32b"); szb = ctx.park("szb")

    xb = ctx.sbuf("p1_xb", [128, NM * T], BF16)
    wi_sb = ctx.sbuf("p1_wi", [128, NM * 4096], BF16)
    bi_sb = ctx.sbuf("p1_bi", [128, 32], F32)
    stg = [ctx.sbuf(f"p1_st{i}", [128, TC], BF16) for i in range(4)]
    st32 = [ctx.sbuf(f"p1_s32{i}", [32, TC], BF16) for i in range(4)]
    pss = [ctx.psum(f"p1_ps{i}", [128, TC], F32) for i in range(2)]

    qs = ctx.sem("p1_qs"); qg = ctx.sem("p1_qg"); qa = ctx.sem("p1_qa")
    mm = ctx.sem("p1_mm"); ev = ctx.sem("p1_ev"); e32 = ctx.sem("p1_e32")

    NG = 32 * NT
    STORES_BEFORE = []
    cnt = 0
    for m in range(32):
        for tc in range(NT):
            cnt += 1 + (1 if m == 0 else 0)
            STORES_BEFORE.append(cnt)

    @block.sync
    def _(sy):
        sy.dma_start(out=bi_sb[:, :], in_=bi).then_inc(qs, 16)

    @block.gpsimd
    def _(g):
        for k in range(NM):
            g.dma_start(out=xb[:, k * T:(k + 1) * T],
                        in_=xf[k * 128:(k + 1) * 128, :]).then_inc(qg, 16)
        for k in range(NM):
            g.dma_start(out=wi_sb[:, k * 4096:(k + 1) * 4096],
                        in_=wiT[k * 128:(k + 1) * 128, :]).then_inc(qg, 16)

    @block.tensor
    def _(t):
        t.wait_ge(qg, 16 * 2 * NM)
        gi = 0
        for m in range(32):
            for tc in range(NT):
                ps = pss[gi % 2]
                if gi >= 2:
                    t.wait_ge(ev, gi - 1)
                for k in range(NM):
                    ins = t.matmul(ps[:, :], wi_sb[:, k * 4096 + 128 * m:k * 4096 + 128 * m + 128],
                                   xb[:, k * T + tc * TC:k * T + tc * TC + TC],
                                   start=(k == 0), stop=(k == NM - 1))
                ins.then_inc(mm, 1)
                gi += 1

    @block.scalar
    def _(s):
        s.wait_ge(qs, 16)
        gi = 0
        for m in range(32):
            for tc in range(NT):
                ps = pss[gi % 2]
                s.wait_ge(mm, gi + 1)
                if gi >= 4:
                    s.wait_ge(qs, 16 * (1 + STORES_BEFORE[gi - 4]))
                buf = stg[gi % 4]
                fn = AF.Identity if m < 16 else AF.Silu
                ins = s.activation(buf[:, :], ps[:, :], fn, bias=bi_sb[:, m:m + 1])
                if m == 0:
                    ins = s.activation(st32[tc % 4][:, :], ps[0:32, :], AF.Identity, bias=bi_sb[0:32, 0:1])
                ins.then_inc(ev, 1)
                gi += 1

    @block.sync
    def _(sy):
        gi = 0
        for m in range(32):
            for tc in range(NT):
                sy.wait_ge(ev, gi + 1)
                buf = stg[gi % 4]
                if m < 16:
                    dst = xinb[m * 128:(m + 1) * 128, tc * TC:tc * TC + TC]
                else:
                    dst = szb[(m - 16) * 128:(m - 16 + 1) * 128, tc * TC:tc * TC + TC]
                sy.dma_start(out=dst, in_=buf[:, :]).then_inc(qs, 16)
                if m == 0:
                    sy.dma_start(out=x32b[:, tc * TC:tc * TC + TC],
                                 in_=st32[tc % 4][:, :]).then_inc(qs, 16)
                gi += 1
        sy.wait_ge(qs, 16 * (1 + NG + NT))



def emit_p2(ctx, block):
    T, TC, NT = ctx.T, ctx.TC, ctx.NT
    wdtT = ctx.inp("wdtT"); bdt = ctx.inp("bdt_c")
    cw = ctx.inp("convw_c"); cb = ctx.inp("convb_c")
    alog = ctx.inp("alog_c"); bm = ctx.inp("bm_c"); cm = ctx.inp("cm_c")
    xinb = ctx.park("xinb"); x32b = ctx.park("x32b"); szb = ctx.park("szb")
    yzb = ctx.park("yzb")

    wdt_sb = ctx.sbuf("p2_wdt", [32, DI], BF16)
    x32sb = ctx.sbuf("p2_x32", [32, T], BF16)
    bdt_sb = ctx.sbuf("p2_bdt", [128, 16], F32)
    cw_sb = ctx.sbuf("p2_cw", [128, 64], F32)
    cb_sb = ctx.sbuf("p2_cb", [128, 16], F32)
    alog_sb = ctx.sbuf("p2_alog", [128, 256], F32)
    bm_sb = ctx.sbuf("p2_bm", [128, 256], F32)
    cm_sb = ctx.sbuf("p2_cm", [128, 256], F32)
    acol = ctx.sbuf("p2_acol", [128, 256], F32)
    cbm = ctx.sbuf("p2_cbm", [128, 256], F32)
    xpad = [ctx.sbuf(f"p2_xp{i}", [128, T + 4], BF16) for i in range(2)]
    szt = [ctx.sbuf(f"p2_sz{i}", [128, T], BF16) for i in range(2)]
    yzo = [ctx.sbuf(f"p2_yz{i}", [128, T], BF16) for i in range(2)]
    upre = ctx.sbuf("p2_upre", [128, T], F32)
    ut = ctx.sbuf("p2_ut", [128, T], F32)
    dtt = ctx.sbuf("p2_dtt", [128, T], F32)
    an = [ctx.sbuf(f"p2_an{i}", [128, T], BF16) for i in range(2)]
    bn = [ctx.sbuf(f"p2_bn{i}", [128, T], BF16) for i in range(2)]
    gn = [ctx.sbuf(f"p2_gn{i}", [128, T], BF16) for i in range(2)]
    yac = ctx.sbuf("p2_yac", [128, T], BF16)
    tmp = [ctx.sbuf(f"p2_tmp{i}", [128, TC], F32) for i in range(4)]
    pdt = [ctx.psum(f"p2_pdt{i}", [128, TC], F32) for i in range(2)]

    qs = ctx.sem("p2_qs"); qg = ctx.sem("p2_qg")
    mm = ctx.sem("p2_mm"); sp = ctx.sem("p2_sp")
    ab = ctx.sem("p2_ab")
    ad = ctx.sem("p2_ad")
    cv = ctx.sem("p2_cv")
    si = ctx.sem("p2_si")
    anr = ctx.sem("p2_an")
    sc = ctx.sem("p2_sc")
    fin = ctx.sem("p2_fin")
    pls = ctx.sem("p2_pls")
    pr = ctx.sem("p2_pr")

    @block.gpsimd
    def _(g):
        g.dma_start(out=wdt_sb[:, :], in_=wdtT).then_inc(qg, 16)
        g.dma_start(out=x32sb[:, :], in_=x32b).then_inc(qg, 16)


    @block.sync
    def _(sy):
        for ap, t in ((bdt, bdt_sb), (cw, cw_sb), (cb, cb_sb), (alog, alog_sb),
                      (bm, bm_sb), (cm, cm_sb)):
            sy.dma_start(out=t[:, :], in_=ap).then_inc(qs, 16)
        for db in range(NDB):
            if db >= 2:
                sy.wait_ge(cv, db - 1)
                sy.wait_ge(fin, db - 1)
            sy.dma_start(out=xpad[db % 2][:, 4:4 + T],
                         in_=xinb[db * 128:(db + 1) * 128, :]).then_inc(qs, 16)
            sy.dma_start(out=szt[db % 2][:, :],
                         in_=szb[db * 128:(db + 1) * 128, :]).then_inc(qs, 16)
            if db >= 1:
                sy.wait_ge(fin, db)
                sy.dma_start(out=yzb[(db - 1) * 128:db * 128, :],
                             in_=yzo[(db - 1) % 2][:, :]).then_inc(qs, 16)
        sy.wait_ge(fin, NDB)
        sy.dma_start(out=yzb[(NDB - 1) * 128:NDB * 128, :],
                     in_=yzo[(NDB - 1) % 2][:, :]).then_inc(qs, 16)
        sy.wait_ge(qs, 16 * (6 + 3 * NDB))

    @block.scalar
    def _(s):
        s.wait_ge(qs, 16 * 6)
        s.activation(acol[:, :], alog_sb[:, :], AF.Exp).then_inc(pr, 1)
        for db in range(NDB):
            for tcq in range(NT):
                i = db * NT + tcq
                s.wait_ge(mm, i + 1)
                if i >= 2:
                    s.wait_ge(ad, i - 1)
                t1 = tmp[(i % 2) * 2]; t2 = tmp[(i % 2) * 2 + 1]
                ps = pdt[i % 2]
                s.activation(t1[:, :], ps[:, :], AF.Abs, bias=bdt_sb[:, db:db + 1])
                s.activation(t1[:, :], t1[:, :], AF.Exp, scale=-1.0)
                s.activation(t1[:, :], t1[:, :], AF.Ln, bias=1.0)
                ins = s.activation(t2[:, :], ps[:, :], AF.Relu, bias=bdt_sb[:, db:db + 1])
                ins.then_inc(ab, 1)
            s.wait_ge(cv, db + 1)
            s.activation(ut[:, :], upre[:, :], AF.Silu, bias=cb_sb[:, db:db + 1]).then_inc(si, 1)
            s.wait_ge(ctx.sem("p2_prv"), 1)
            s.wait_ge(ad, (db + 1) * NT)
            for n in range(DS):
                ni = db * DS + n
                if ni >= 2:
                    s.wait_ge(sc, ni - 1)
                s.activation(an[ni % 2][:, :], dtt[:, :], AF.Exp,
                             scale=acol[:, db * 16 + n:db * 16 + n + 1]).then_inc(anr, 1)

    @block.tensor
    def _(t):
        t.wait_ge(qg, 32)
        for db in range(NDB):
            for tcq in range(NT):
                i = db * NT + tcq
                if i >= 2:
                    t.wait_ge(ab, i - 1)
                t.matmul(pdt[i % 2][:, :], wdt_sb[:, db * 128:(db + 1) * 128],
                         x32sb[:, tcq * TC:tcq * TC + TC],
                         start=True, stop=True).then_inc(mm, 1)

    @block.vector
    def _(v):
        v.wait_ge(pr, 1)
        v.tensor_scalar(acol[:, :], acol[:, :], -1.0, None, ALU.mult)
        v.wait_ge(ctx.sem("p2_qs"), 16 * 6)
        v.tensor_mul(cbm[:, :], cm_sb[:, :], bm_sb[:, :]).then_inc(ctx.sem("p2_prv"), 1)
        for db in range(NDB):
            v.wait_ge(qs, 16 * (6 + 2 * (db + 1)))
            if db >= 1:
                v.wait_ge(si, db)
            v.memset(xpad[db % 2][:, 0:4], 0.0)
            v.tensor_scalar(upre[:, :], xpad[db % 2][:, 4:4 + T],
                            cw_sb[:, db * 4 + 3:db * 4 + 4], None, ALU.mult)
            for o in (1, 2, 3):
                ins = v.scalar_tensor_tensor(upre[:, :], xpad[db % 2][:, 4 - o:4 - o + T],
                                             cw_sb[:, db * 4 + 3 - o:db * 4 + 4 - o],
                                             upre[:, :], ALU.mult, ALU.add)
            ins.then_inc(cv, 1)
            for tcq in range(NT):
                i = db * NT + tcq
                v.wait_ge(ab, i + 1)
                v.tensor_add(dtt[:, tcq * TC:tcq * TC + TC],
                             tmp[(i % 2) * 2][:, :], tmp[(i % 2) * 2 + 1][:, :]).then_inc(ad, 1)
            v.wait_ge(si, db + 1)
            for n in range(DS):
                ni = db * DS + n
                v.scalar_tensor_tensor(bn[ni % 2][:, :], dtt[:, :],
                                       cbm[:, db * 16 + n:db * 16 + n + 1],
                                       ut[:, :], ALU.mult, ALU.mult)
                v.wait_ge(anr, ni + 1)
                v.tensor_tensor_scan(gn[ni % 2][:, :], an[ni % 2][:, :], bn[ni % 2][:, :],
                                     0.0, ALU.mult, ALU.add).then_inc(sc, 1)
                if n == 0:
                    v.tensor_copy(yac[:, :], gn[ni % 2][:, :])
                else:
                    v.tensor_add(yac[:, :], yac[:, :], gn[ni % 2][:, :])
            if db >= 2:
                v.wait_ge(qs, 16 * (3 * db + 5))
            v.tensor_mul(yzo[db % 2][:, :], yac[:, :], szt[db % 2][:, :]).then_inc(fin, 1)



def emit_p3(ctx, block):
    T, TC, NT = ctx.T, ctx.TC, ctx.NT
    xf = ctx.inp("xf"); woT = ctx.inp("woT"); bo = ctx.inp("bo_c")
    onc = ctx.inp("ones_col"); onr = ctx.inp("ones_row")
    yzb = ctx.park("yzb"); dvm1 = ctx.park("dvm1")
    x1f = ctx.park("x1f"); x1b = ctx.park("x1b")

    wo_sb = ctx.sbuf("p3_wo", [128, NDB * 1024], BF16)
    xff = [ctx.sbuf(f"p3_xf{i}", [128, NM * TC], F32) for i in range(2)]
    yzt = [ctx.sbuf(f"p3_yz{i}", [128, NDB * TC], BF16) for i in range(2)]
    bo_sb = ctx.sbuf("p3_bo", [128, 8], F32)
    dvm_sb = ctx.sbuf("p3_dvm", [128, 1], F32)
    onc_sb = ctx.sbuf("p3_onc", [128, 1], BF16)
    onr_sb = ctx.sbuf("p3_onr", [1, 512], BF16)
    tmpE = [ctx.sbuf(f"p3_te{i}", [128, TC], F32) for i in range(2)]
    res = ctx.sbuf("p3_res", [128, NM * TC], F32)
    resb = ctx.sbuf("p3_resb", [128, NM * TC], BF16)
    sqb = ctx.sbuf("p3_sqb", [128, NM * TC], BF16)
    xstg = [ctx.sbuf(f"p3_xs{i}", [128, TC], F32) for i in range(4)]
    mrow = ctx.sbuf("p3_mrow", [1, TC], F32)
    vrow = ctx.sbuf("p3_vrow", [1, TC], F32)
    srow = ctx.sbuf("p3_srow", [1, TC], F32)
    rrow = ctx.sbuf("p3_rrow", [1, TC], F32)
    msq = ctx.sbuf("p3_msq", [1, TC], F32)
    mrow_b = ctx.sbuf("p3_mrb", [1, TC], BF16)
    rrow_b = ctx.sbuf("p3_rrb", [1, TC], BF16)
    psA = [ctx.psum(f"p3_pa{i}", [128, TC], F32) for i in range(2)]
    psS = ctx.psum("p3_ps", [1, TC], F32)
    psQ = ctx.psum("p3_pq", [1, TC], F32)
    psM = ctx.psum("p3_pm", [128, TC], F32)
    psR = ctx.psum("p3_pr", [128, TC], F32)

    qs = ctx.sem("p3_qs"); qg = ctx.sem("p3_qg")
    mm = ctx.sem("p3_mm"); st = ctx.sem("p3_st"); bc = ctx.sem("p3_bc")
    ae = ctx.sem("p3_ae"); sq = ctx.sem("p3_sq")
    ar1 = ctx.sem("p3_ar1"); ar2 = ctx.sem("p3_ar2")
    rv = ctx.sem("p3_rv"); rb = ctx.sem("p3_rb"); dv2 = ctx.sem("p3_dv2")
    rw = ctx.sem("p3_rw"); xn = ctx.sem("p3_xn")

    @block.sync
    def _(sy):
        for ap, tile in ((dvm1, dvm_sb), (bo, bo_sb)):
            sy.dma_start(out=tile[:, :], in_=ap).then_inc(qs, 16)
        sy.dma_start(out=onc_sb[:, :], in_=onc).then_inc(qs, 16)
        sy.dma_start(out=onr_sb[:, :], in_=onr).then_inc(qs, 16)
        for tc in range(NT):
            if tc >= 2:
                sy.wait_ge(rv, 8 * (tc - 1))
                sy.wait_ge(mm, 8 * (tc - 1))
            for m in range(NM):
                sy.dma_start(out=xff[tc % 2][:, m * TC:(m + 1) * TC],
                             in_=xf[m * 128:(m + 1) * 128, tc * TC:tc * TC + TC]).then_inc(qs, 16)
            for k in range(NDB):
                sy.dma_start(out=yzt[tc % 2][:, k * TC:(k + 1) * TC],
                             in_=yzb[k * 128:(k + 1) * 128, tc * TC:tc * TC + TC]).then_inc(qs, 16)

    @block.gpsimd
    def _(g):
        for k in range(NDB):
            g.dma_start(out=wo_sb[:, k * 1024:(k + 1) * 1024],
                        in_=woT[k * 128:(k + 1) * 128, :]).then_inc(qg, 16)
        for tc in range(NT):
            for m in range(NM):
                idx = tc * NM + m
                g.wait_ge(xn, idx + 1)
                b = xstg[idx % 4]
                g.dma_start(out=x1b[m * 128:(m + 1) * 128, tc * TC:tc * TC + TC],
                            in_=b[:, :]).then_inc(qg, 16)
                g.dma_start(out=x1f[m * 128:(m + 1) * 128, tc * TC:tc * TC + TC],
                            in_=b[:, :]).then_inc(qg, 16)
        g.wait_ge(qg, 16 * (NDB + 2 * NM * NT))

    @block.tensor
    def _(t):
        t.wait_ge(qg, 16 * NDB)
        for tc in range(NT):
            t.wait_ge(qs, 16 * (4 + (NM + NDB) * (tc + 1)))
            for m in range(NM):
                idx = tc * NM + m
                if idx >= 2:
                    t.wait_ge(ae, idx - 1)
                for k in range(NDB):
                    ins = t.matmul(psA[idx % 2][:, :],
                                   wo_sb[:, k * 1024 + 128 * m:k * 1024 + 128 * m + 128],
                                   yzt[tc % 2][:, k * TC:(k + 1) * TC],
                                   start=(k == 0), stop=(k == NDB - 1))
                ins.then_inc(mm, 1)
            t.wait_ge(rb, tc + 1)
            t.wait_ge(sq, tc + 1)
            for m in range(NM):
                t.matmul(psS[:, :], onc_sb[:, :], resb[:, m * TC:(m + 1) * TC],
                         start=(m == 0), stop=(m == NM - 1))
            for m in range(NM):
                ins = t.matmul(psQ[:, :], onc_sb[:, :], sqb[:, m * TC:(m + 1) * TC],
                               start=(m == 0), stop=(m == NM - 1))
            ins.then_inc(st, 1)
            t.wait_ge(rw, tc + 1)
            t.matmul(psM[:, :], onr_sb[0:1, 0:128], mrow_b[:, :], start=True, stop=True)
            t.matmul(psR[:, :], onr_sb[0:1, 0:128], rrow_b[:, :], start=True, stop=True).then_inc(bc, 1)

    @block.scalar
    def _(s):
        s.wait_ge(qs, 64)
        for tc in range(NT):
            for m in range(NM):
                idx = tc * NM + m
                s.wait_ge(mm, idx + 1)
                if idx >= 2:
                    s.wait_ge(rv, idx - 1)
                s.activation(tmpE[idx % 2][:, :], psA[idx % 2][:, :], AF.Identity,
                             bias=bo_sb[:, m:m + 1]).then_inc(ae, 1)
            s.wait_ge(rb, tc + 1)
            s.activation(sqb[:, :], resb[:, :], AF.Square).then_inc(sq, 1)
            s.wait_ge(st, tc + 1)
            s.activation(mrow[:, :], psS[:, :], AF.Identity, scale=1.0 / DM)
            s.activation(vrow[:, :], psQ[:, :], AF.Identity, scale=1.0 / DM).then_inc(ar1, 1)
            s.wait_ge(dv2, tc + 1)
            s.activation(srow[:, :], vrow[:, :], AF.Sqrt).then_inc(ar2, 1)

    @block.vector
    def _(v):
        v.wait_ge(qs, 32)
        for tc in range(NT):
            for m in range(NM):
                idx = tc * NM + m
                v.wait_ge(ae, idx + 1)
                v.scalar_tensor_tensor(res[:, m * TC:(m + 1) * TC],
                                       xff[tc % 2][:, m * TC:(m + 1) * TC],
                                       dvm_sb[:, 0:1], tmpE[idx % 2][:, :],
                                       ALU.mult, ALU.add).then_inc(rv, 1)
                ins = v.tensor_copy(resb[:, m * TC:(m + 1) * TC], res[:, m * TC:(m + 1) * TC])
            ins.then_inc(rb, 1)
            v.wait_ge(ar1, tc + 1)
            v.tensor_mul(msq[:, :], mrow[:, :], mrow[:, :])
            v.scalar_tensor_tensor(vrow[:, :], vrow[:, :], 1e-5, msq[:, :],
                                   ALU.add, ALU.subtract).then_inc(dv2, 1)
            v.wait_ge(ar2, tc + 1)
            v.reciprocal(rrow[:, :], srow[:, :])
            v.tensor_copy(mrow_b[:, :], mrow[:, :])
            v.tensor_copy(rrow_b[:, :], rrow[:, :]).then_inc(rw, 1)
            v.wait_ge(bc, tc + 1)
            for m in range(NM):
                idx = tc * NM + m
                if idx >= 4:
                    v.wait_ge(qg, 16 * (NDB + 2 * (idx - 3)))
                v.tensor_sub(xstg[idx % 4][:, :], res[:, m * TC:(m + 1) * TC], psM[:, :])
                v.tensor_mul(xstg[idx % 4][:, :], xstg[idx % 4][:, :], psR[:, :]).then_inc(xn, 1)



def emit_ln_sum(ctx, block, pref, parkA, parkB, dstF, dstB, pre_qg=0, pre_qs=0):
    T, TC, NT = ctx.T, ctx.TC, ctx.NT
    onc = ctx.inp("ones_col"); onr = ctx.inp("ones_row")
    nc = ctx.nc
    av = [ctx.sbuf(f"{pref}_a{i}", [128, NM * TC], F32) for i in range(2)]
    bv = [ctx.sbuf(f"{pref}_b{i}", [128, NM * TC], F32) for i in range(2)]
    res = ctx.sbuf(f"{pref}_res", [128, NM * TC], F32)
    resb = ctx.sbuf(f"{pref}_resb", [128, NM * TC], BF16)
    sqb = ctx.sbuf(f"{pref}_sqb", [128, NM * TC], BF16)
    xstg = [ctx.sbuf(f"{pref}_xs{i}", [128, TC], F32) for i in range(4)]
    onc_sb = ctx.sbuf(f"{pref}_onc", [128, 1], BF16)
    onr_sb = ctx.sbuf(f"{pref}_onr", [1, 512], BF16)
    mrow = ctx.sbuf(f"{pref}_mrow", [1, TC], F32)
    vrow = ctx.sbuf(f"{pref}_vrow", [1, TC], F32)
    srow = ctx.sbuf(f"{pref}_srow", [1, TC], F32)
    rrow = ctx.sbuf(f"{pref}_rrow", [1, TC], F32)
    msq = ctx.sbuf(f"{pref}_msq", [1, TC], F32)
    mrow_b = ctx.sbuf(f"{pref}_mrb", [1, TC], BF16)
    rrow_b = ctx.sbuf(f"{pref}_rrb", [1, TC], BF16)
    psS = [ctx.psum(f"{pref}_ps{i}", [1, TC], F32) for i in range(2)]
    psQ = [ctx.psum(f"{pref}_pq{i}", [1, TC], F32) for i in range(2)]
    psM = [ctx.psum(f"{pref}_pm{i}", [128, TC], F32) for i in range(2)]
    psR = [ctx.psum(f"{pref}_pr{i}", [128, TC], F32) for i in range(2)]

    qs = ctx.sem(f"{pref}_qs2"); qg = ctx.sem(f"{pref}_qg2")
    st = ctx.sem(f"{pref}_st"); bc = ctx.sem(f"{pref}_bc")
    sq = ctx.sem(f"{pref}_sq"); ar1 = ctx.sem(f"{pref}_ar1"); ar2 = ctx.sem(f"{pref}_ar2")
    rb = ctx.sem(f"{pref}_rb"); dv2 = ctx.sem(f"{pref}_dv2")
    rw = ctx.sem(f"{pref}_rw"); xn = ctx.sem(f"{pref}_xn")
    nst = 2 if dstB is not None else 1

    @block.sync
    def _(sy):
        sy.dma_start(out=onc_sb[:, :], in_=onc).then_inc(qs, 16)
        sy.dma_start(out=onr_sb[:, :], in_=onr).then_inc(qs, 16)
        for tc in range(NT):
            if tc >= 2:
                sy.wait_ge(rb, tc - 1)
            for m in range(NM):
                sy.dma_start(out=av[tc % 2][:, m * TC:(m + 1) * TC],
                             in_=parkA[m * 128:(m + 1) * 128, tc * TC:tc * TC + TC]).then_inc(qs, 16)
                sy.dma_start(out=bv[tc % 2][:, m * TC:(m + 1) * TC],
                             in_=parkB[m * 128:(m + 1) * 128, tc * TC:tc * TC + TC]).then_inc(qs, 16)

    @block.tensor
    def _(t):
        for tc in range(NT):
            t.wait_ge(rb, tc + 1)
            t.wait_ge(sq, tc + 1)
            if tc >= 2:
                t.wait_ge(ar1, tc - 1)
            for m in range(NM):
                t.matmul(psS[tc % 2][:, :], onc_sb[:, :], resb[:, m * TC:(m + 1) * TC],
                         start=(m == 0), stop=(m == NM - 1))
            for m in range(NM):
                ins = t.matmul(psQ[tc % 2][:, :], onc_sb[:, :], sqb[:, m * TC:(m + 1) * TC],
                               start=(m == 0), stop=(m == NM - 1))
            ins.then_inc(st, 1)
            t.wait_ge(rw, tc + 1)
            if tc >= 2:
                t.wait_ge(xn, NM * (tc - 1))
            t.matmul(psM[tc % 2][:, :], onr_sb[0:1, 0:128], mrow_b[:, :], start=True, stop=True)
            t.matmul(psR[tc % 2][:, :], onr_sb[0:1, 0:128], rrow_b[:, :], start=True, stop=True).then_inc(bc, 1)

    @block.scalar
    def _(s):
        for tc in range(NT):
            s.wait_ge(rb, tc + 1)
            s.activation(sqb[:, :], resb[:, :], AF.Square).then_inc(sq, 1)
            s.wait_ge(st, tc + 1)
            s.activation(mrow[:, :], psS[tc % 2][:, :], AF.Identity, scale=1.0 / DM)
            s.activation(vrow[:, :], psQ[tc % 2][:, :], AF.Identity, scale=1.0 / DM).then_inc(ar1, 1)
            s.wait_ge(dv2, tc + 1)
            s.activation(srow[:, :], vrow[:, :], AF.Sqrt).then_inc(ar2, 1)

    @block.vector
    def _(v):
        for tc in range(NT):
            v.wait_ge(qs, pre_qs + 16 * (2 + 2 * NM * (tc + 1)))
            v.tensor_add(res[:, :], av[tc % 2][:, :], bv[tc % 2][:, :])
            v.tensor_copy(resb[:, :], res[:, :]).then_inc(rb, 1)
            v.wait_ge(ar1, tc + 1)
            v.tensor_mul(msq[:, :], mrow[:, :], mrow[:, :])
            v.scalar_tensor_tensor(vrow[:, :], vrow[:, :], 1e-5, msq[:, :],
                                   ALU.add, ALU.subtract).then_inc(dv2, 1)
            v.wait_ge(ar2, tc + 1)
            v.reciprocal(rrow[:, :], srow[:, :])
            v.tensor_copy(mrow_b[:, :], mrow[:, :])
            v.tensor_copy(rrow_b[:, :], rrow[:, :]).then_inc(rw, 1)
            v.wait_ge(bc, tc + 1)
            for m in range(NM):
                idx = tc * NM + m
                if idx >= 4:
                    v.wait_ge(qg, pre_qg + 16 * nst * (idx - 3))
                v.tensor_sub(xstg[idx % 4][:, :], res[:, m * TC:(m + 1) * TC], psM[tc % 2][:, :])
                v.tensor_mul(xstg[idx % 4][:, :], xstg[idx % 4][:, :], psR[tc % 2][:, :]).then_inc(xn, 1)

    @block.gpsimd
    def _(g):
        for tc in range(NT):
            for m in range(NM):
                idx = tc * NM + m
                g.wait_ge(xn, idx + 1)
                b = xstg[idx % 4]
                g.dma_start(out=dstF[m * 128:(m + 1) * 128, tc * TC:tc * TC + TC],
                            in_=b[:, :]).then_inc(qg, 16)
                if dstB is not None:
                    g.dma_start(out=dstB[m * 128:(m + 1) * 128, tc * TC:tc * TC + TC],
                                in_=b[:, :]).then_inc(qg, 16)
        g.wait_ge(qg, pre_qg + 16 * nst * NM * NT)



def emit_p4(ctx, block):
    T, TC, NT = ctx.T, ctx.TC, ctx.NT
    NTT = T // 128
    wg = ctx.inp("wg"); bg = ctx.inp("bg_c")
    w1 = ctx.inp("w1"); b1 = ctx.inp("b1_c"); w2 = ctx.inp("w2"); b2 = ctx.inp("b2_c")
    identf = ctx.inp("identf"); onr = ctx.inp("ones_row")
    x1f = ctx.park("x1f"); x1b = ctx.park("x1b")
    wrow = ctx.park("wrow"); facc = ctx.park("facc")
    x2f = ctx.park("x2f"); x2b = ctx.park("x2b")

    qs = ctx.sem("p4_qs"); qg = ctx.sem("p4_qg")

    _gate_scope = ctx.nc.named_scope("p4gate")
    _gate_scope.__enter__()
    gstk = ExitStack()
    sb = lambda n, sh, dt=F32: gstk.enter_context(ctx.nc.sbuf_tensor(n, sh, dt))
    psm = lambda n, sh: gstk.enter_context(ctx.nc.psum_tensor(n, sh, F32))

    wg_sb = ctx.sbuf("p4_wg", [128, NM * 8], F32)
    bg_sb = ctx.sbuf("p4_bg", [8, 1], F32)
    onr_sb = ctx.sbuf("p4_onr", [1, 512], BF16)
    x1t = [ctx.sbuf(f"p4_x1t{i}", [128, NM * TC], BF16) for i in range(2)]
    wbc = ctx.sbuf("p4_wbc", [128, T], BF16)

    x1g = [gstk.enter_context(ctx.nc.sbuf_tensor(f"p4_x1g{i}", [128, NM * TC], F32)) for i in range(2)]
    idf_sb = sb("p4_idf", [128, 128])
    lgs = sb("p4_lgs", [8, T])
    ltok = sb("p4_ltok", [128, 8 * NTT])
    wtok = sb("p4_wtok", [128, 8 * NTT])
    srt = sb("p4_srt", [128, 8 * NTT])
    shf1 = sb("p4_shf1", [128, 8 * NTT]); shf2 = sb("p4_shf2", [128, 8 * NTT])
    eea = sb("p4_eea", [128, 8 * NTT]); sela = sb("p4_sela", [128, 8 * NTT])
    dena = sb("p4_dena", [128, NTT]); rdena = sb("p4_rdena", [128, NTT])
    red4 = sb("p4_red4", [128, 4 * NTT]); red2 = sb("p4_red2", [128, 2 * NTT])
    dmy = sb("p4_dmy", [128, 1])
    NBW = (8 * NTT + 127) // 128
    wtr = sb("p4_wtr", [128, NBW * 128])
    psG = psm("p4_psG", [8, TC])
    psTT = psm("p4_psTT", [128, 8 * NTT])
    psT2T = psm("p4_psT2T", [128, NBW * 128])
    psD = psm("p4_psD", [128, 128])

    gmm = ctx.sem("p4_gmm"); gae = ctx.sem("p4_gae")
    gtp = ctx.sem("p4_gtp"); gdv = ctx.sem("p4_gdv")
    gex = ctx.sem("p4_gex"); gxd = ctx.sem("p4_gxd")
    gwv = ctx.sem("p4_gwv"); gt2 = ctx.sem("p4_gt2"); gw2 = ctx.sem("p4_gw2")

    @block.sync
    def _(sy):
        sy.dma_start(out=bg_sb[:, :], in_=bg).then_inc(qs, 16)
        sy.dma_start(out=idf_sb[:, :], in_=identf).then_inc(qs, 16)
        for tc in range(NT):
            if tc >= 2:
                sy.wait_ge(gmm, tc - 1)
            for m in range(NM):
                sy.dma_start(out=x1g[tc % 2][:, m * TC:(m + 1) * TC],
                             in_=x1f[m * 128:(m + 1) * 128, tc * TC:tc * TC + TC]).then_inc(qs, 16)
        sy.wait_ge(gw2, 1)
        for i in range(NTT):
            c = (8 * i) // 128
            r = 8 * i - 128 * c
            sy.dma_start(out=wrow[:, i * 128:(i + 1) * 128],
                         in_=wtr[r:r + 8, c * 128:(c + 1) * 128]).then_inc(qs, 16)

    @block.gpsimd
    def _(g):
        for m in range(NM):
            g.dma_start(out=wg_sb[:, m * 8:(m + 1) * 8],
                        in_=wg[m * 128:(m + 1) * 128, :]).then_inc(qg, 16)
        g.dma_start(out=onr_sb[:, :], in_=onr).then_inc(qg, 16)

    @block.tensor
    def _(t):
        t.wait_ge(qg, 16 * (NM + 1))
        for tc in range(NT):
            t.wait_ge(qs, 16 * (2 + NM * (tc + 1)))
            if tc >= 1:
                t.wait_ge(gae, tc)
            for m in range(NM):
                ins = t.matmul(psG[:, :], wg_sb[:, m * 8:(m + 1) * 8],
                               x1g[tc % 2][:, m * TC:(m + 1) * TC],
                               start=(m == 0), stop=(m == NM - 1))
            ins.then_inc(gmm, 1)
        for i in range(NTT):
            t.wait_ge(gae, (i * 128) // TC + 1)
            t.transpose(psTT[:, 8 * i:8 * i + 8], lgs[:, i * 128:(i + 1) * 128],
                        idf_sb[0:8, 0:8]).then_inc(gtp, 1)
        t.transpose(psD[:, 0:8], lgs[:, 0:128], idf_sb[0:8, 0:8]).then_inc(gtp, 1)
        t.wait_ge(gwv, 1)
        for c in range(NBW):
            wc = min(128, 8 * NTT - c * 128)
            t.transpose(psT2T[0:wc, c * 128:(c + 1) * 128],
                        wtok[:, c * 128:c * 128 + wc],
                        idf_sb[:, :]).then_inc(gt2, 1)
        t.transpose(psD[0:8, :], wtok[:, 0:8], idf_sb[:, :]).then_inc(gt2, 1)

    @block.scalar
    def _(s):
        s.wait_ge(qs, 16)
        for tc in range(NT):
            s.wait_ge(gmm, tc + 1)
            s.activation(lgs[:, tc * TC:tc * TC + TC], psG[:, :], AF.Identity,
                         bias=bg_sb[:, 0:1]).then_inc(gae, 1)
        s.wait_ge(gxd, 1)
        s.activation(dmy[:, :], dmy[:, :], AF.Identity)
        s.activation(dmy[:, :], dmy[:, :], AF.Identity)
        s.activation(eea[:, :], shf1[:, :], AF.Exp)
        s.activation(sela[:, :], shf2[:, :], AF.Sign).then_inc(gex, 1)

    @block.vector
    def _(v):
        v.memzero(dmy[:, :])
        v.wait_ge(gtp, NTT + 1)
        v.tensor_copy(ltok[:, :], psTT[:, :])
        lt3 = ltok[:, :].rearrange("p (n k) -> p n k", k=8)
        srt3 = srt[:, :].rearrange("p (n k) -> p n k", k=8)
        for i in range(NTT):
            v.max(out=srt[:, 8 * i:8 * i + 8], in_=ltok[:, 8 * i:8 * i + 8])
        v.tensor_copy(dmy[:, :], dmy[:, :])
        v.tensor_copy(dmy[:, :], dmy[:, :])
        v.tensor_copy(dmy[:, :], dmy[:, :])
        v.tensor_sub(shf1[:, :].rearrange("p (n k) -> p n k", k=8), lt3,
                     srt3[:, :, 0:1].broadcast_to([128, NTT, 8]))
        v.tensor_sub(shf2[:, :].rearrange("p (n k) -> p n k", k=8), lt3,
                     srt3[:, :, 1:2].broadcast_to([128, NTT, 8]))
        v.tensor_copy(dmy[:, :], shf2[:, 0:1]).then_inc(gxd, 1)
        v.wait_ge(gex, 1)

        def vdly():
            v.tensor_copy(dmy[:, :], dmy[:, :])
            v.tensor_copy(dmy[:, :], dmy[:, :])
            v.tensor_copy(dmy[:, :], dmy[:, :])

        vdly()
        v.tensor_scalar(sela[:, :], sela[:, :], 1.0, 1.0, ALU.add, ALU.min)
        vdly()
        v.tensor_mul(sela[:, :], sela[:, :], eea[:, :])
        vdly()

        def tree_red(src_t, dst, op):
            s3 = src_t[:, :].rearrange("p (n k) -> p n k", k=8)
            r43 = red4[:, :].rearrange("p (n k) -> p n k", k=4)
            r23 = red2[:, :].rearrange("p (n k) -> p n k", k=2)
            v.tensor_tensor(r43, s3[:, :, 0:8:2], s3[:, :, 1:8:2], op)
            vdly()
            v.tensor_tensor(r23, r43[:, :, 0:4:2], r43[:, :, 1:4:2], op)
            vdly()
            v.tensor_tensor(dst[:, :].rearrange("p (n k) -> p n k", k=1),
                            r23[:, :, 0:2:2], r23[:, :, 1:2:2], op)
            vdly()

        tree_red(sela, dena, ALU.add)
        v.reciprocal(rdena[:, :], dena[:, :])
        vdly()
        v.tensor_mul(wtok[:, :].rearrange("p (n k) -> p n k", k=8),
                     sela[:, :].rearrange("p (n k) -> p n k", k=8),
                     rdena[:, :].unsqueeze(2).broadcast_to([128, NTT, 8]))
        vdly()
        v.tensor_copy(dmy[:, :], wtok[:, 0:1]).then_inc(gwv, 1)
        v.wait_ge(gt2, NBW + 1)
        for c in range(NBW):
            wc = min(128, 8 * NTT - c * 128)
            ins = v.tensor_copy(wtr[0:wc, c * 128:(c + 1) * 128],
                                psT2T[0:wc, c * 128:(c + 1) * 128])
        ins.then_inc(gw2, 1)

    if DBG_GATE:
        dbgt = {}
        for n, tile in (("ltok", ltok), ("srt", srt), ("eea", eea), ("sela", sela),
                        ("wtok", wtok), ("dena", dena), ("rdena", rdena), ("lgs", lgs)):
            dbgt[n] = (ctx.nc.dram_tensor("dbg_" + n, list(tile.shape), F32,
                                          kind="ExternalOutput").ap(), tile)

        @block.sync
        def _(sy):
            sy.wait_ge(ctx.sem("p4_gw2"), 1)
            sy.nop(cycle_cnt=2000)
            for n, (ap, tile) in dbgt.items():
                sy.dma_start(out=ap, in_=tile[:, :]).then_inc(qs, 16)
            sy.wait_ge(qs, 16 * (2 + NM * NT + NTT + len(dbgt)))
        QS0_EXTRA = len(dbgt)
    else:
        QS0_EXTRA = 0

    gstk.close()
    _gate_scope.__exit__(None, None, None)
    _exp_scope = ctx.nc.named_scope("p4exp")
    _exp_scope.__enter__()

    estk = ExitStack()
    sbe = lambda n, sh, dt: estk.enter_context(ctx.nc.sbuf_tensor(n, sh, dt))
    psme = lambda n, sh: estk.enter_context(ctx.nc.psum_tensor(n, sh, F32))

    w1_sb = sbe("p4_w1", [128, NM * FF], BF16)
    w2_sb = sbe("p4_w2", [128, 32 * DM], BF16)
    b1_sb = [sbe(f"p4_b1{i}", [128, 32], F32) for i in range(2)]
    b2_sb = [sbe(f"p4_b2{i}", [128, 8], F32) for i in range(2)]
    web = sbe("p4_web", [1, T], BF16)
    hsb = sbe("p4_hsb", [128, 32 * TC], BF16)
    yt = [sbe(f"p4_yt{i}", [128, TC], F32) for i in range(2)]
    ysc = [sbe(f"p4_ysc{i}", [128, TC], F32) for i in range(4)]
    pH = [psme(f"p4_pH{i}", [128, TC]) for i in range(2)]
    pY = [psme(f"p4_pY{i}", [128, TC]) for i in range(2)]
    pW = [psme(f"p4_pW{i}", [128, TC]) for i in range(2)]

    hmm = ctx.sem("p4_hmm"); hv = ctx.sem("p4_hv")
    ymm = ctx.sem("p4_ymm"); yae = ctx.sem("p4_yae"); ydv = ctx.sem("p4_ydv")
    wbm = ctx.sem("p4_wbm"); wbe = ctx.sem("p4_wbe")

    QS0 = 2 + NM * NT + NTT + QS0_EXTRA
    SE = NM * NT + 2
    QG0 = NM + 1
    GE = 41 + NM * NT

    QG_W1 = {}; QG_W2 = {}; QG_WEB = {}; QG_ACC = {}
    _c = QG0
    for e in range(NE_RUN):
        if e == 0:
            _c += NM; QG_W1[0] = _c
        _c += 32; QG_W2[e] = _c
        _c += 1; QG_WEB[e] = _c
        for tc in range(NT):
            for dm in range(NM):
                slot = (e * NT + tc) * NM + dm
                _c += 1; QG_ACC[slot] = _c
                if tc == NT - 1 and dm == 0 and e + 1 < NE_RUN:
                    _c += NM; QG_W1[e + 1] = _c
    QG_TOT = _c

    @block.gpsimd
    def _(g):
        for e in range(NE_RUN):
            if e == 0:
                g.wait_ge(ctx.sem("p4_gw2"), 1)
                g.wait_ge(ctx.sem("p4_qs"), 16 * QS0)
                for k in range(NM):
                    g.dma_start(out=w1_sb[:, k * FF:(k + 1) * FF],
                                in_=w1[e, k * 128:(k + 1) * 128, :]).then_inc(qg, 16)
            if e >= 1:
                g.wait_ge(ymm, 8 * NT * e)
            for k in range(32):
                g.dma_start(out=w2_sb[:, k * DM:(k + 1) * DM],
                            in_=w2[e, k * 128:(k + 1) * 128, :]).then_inc(qg, 16)
            if e >= 1:
                g.wait_ge(wbm, NT * e)
            g.dma_start(out=web[:, :], in_=wrow[e:e + 1, :]).then_inc(qg, 16)
            for tc in range(NT):
                for dm in range(NM):
                    slot = (e * NT + tc) * NM + dm
                    g.wait_ge(ydv, slot + 1)
                    op = {} if e == 0 else {"accum_op": ALU.add}
                    g.dma_start(out=facc[dm * 128:(dm + 1) * 128, tc * TC:tc * TC + TC],
                                in_=ysc[slot % 4][:, :], **op).then_inc(qg, 16)
                    if tc == NT - 1 and dm == 0 and e + 1 < NE_RUN:
                        g.wait_ge(hmm, 32 * NT * (e + 1))
                        for k in range(NM):
                            g.dma_start(out=w1_sb[:, k * FF:(k + 1) * FF],
                                        in_=w1[e + 1, k * 128:(k + 1) * 128, :]).then_inc(qg, 16)

    @block.sync
    def _(sy):
        for e in range(NE_RUN):
            if e >= 2:
                sy.wait_ge(hv, 32 * NT * (e - 1))
                sy.wait_ge(yae, NM * NT * (e - 1))
            sy.dma_start(out=b1_sb[e % 2][:, :], in_=b1[e]).then_inc(qs, 16)
            sy.dma_start(out=b2_sb[e % 2][:, :], in_=b2[e]).then_inc(qs, 16)
            for tc in range(NT):
                if e * NT + tc >= 2:
                    sy.wait_ge(hmm, 32 * (e * NT + tc - 1))
                for m in range(NM):
                    sy.dma_start(out=x1t[(e * NT + tc) % 2][:, m * TC:(m + 1) * TC],
                                 in_=x1b[m * 128:(m + 1) * 128, tc * TC:tc * TC + TC]).then_inc(qs, 16)

    @block.tensor
    def _(t):
        for e in range(NE_RUN):
            t.wait_ge(qg, 16 * max(QG_W1[e], QG_WEB[e]))
            for tc in range(NT):
                if e * NT + tc >= 2:
                    t.wait_ge(wbe, e * NT + tc - 1)
                t.matmul(pW[(e * NT + tc) % 2][:, :], onr_sb[0:1, 0:128],
                         web[0:1, tc * TC:tc * TC + TC], start=True, stop=True).then_inc(wbm, 1)
            for tc in range(NT):
                gi = e * NT + tc
                t.wait_ge(qs, 16 * (QS0 + SE * e + 2 + NM * (tc + 1)))
                for f in range(32):
                    fi = gi * 32 + f
                    if fi >= 2:
                        t.wait_ge(hv, fi - 1)
                    for k in range(NM):
                        ins = t.matmul(pH[fi % 2][:, :],
                                       w1_sb[:, k * FF + 128 * f:k * FF + 128 * f + 128],
                                       x1t[gi % 2][:, k * TC:(k + 1) * TC],
                                       start=(k == 0), stop=(k == NM - 1))
                    ins.then_inc(hmm, 1)
                t.wait_ge(hv, gi * 32 + 32)
                for dm in range(NM):
                    slot = gi * NM + dm
                    if slot >= 2:
                        t.wait_ge(yae, slot - 1)
                    for k in range(32):
                        ins = t.matmul(pY[slot % 2][:, :],
                                       w2_sb[:, k * DM + 128 * dm:k * DM + 128 * dm + 128],
                                       hsb[:, k * TC:(k + 1) * TC],
                                       start=(k == 0), stop=(k == 31))
                    ins.then_inc(ymm, 1)

    @block.scalar
    def _(s):
        for e in range(NE_RUN):
            s.wait_ge(ydv, NM * NT * e)
            s.wait_ge(qs, 16 * (QS0 + SE * e + 2))
            for tc in range(NT):
                gi = e * NT + tc
                s.wait_ge(wbm, gi + 1)
                s.activation(wbc[:, tc * TC:tc * TC + TC], pW[gi % 2][:, :],
                             AF.Identity).then_inc(wbe, 1)
            for tc in range(NT):
                gi = e * NT + tc
                s.wait_ge(ymm, NM * gi)
                for f in range(32):
                    fi = gi * 32 + f
                    s.wait_ge(hmm, fi + 1)
                    s.activation(hsb[:, f * TC:(f + 1) * TC], pH[fi % 2][:, :],
                                 AF.Gelu, bias=b1_sb[e % 2][:, f:f + 1]).then_inc(hv, 1)
                for dm in range(NM):
                    slot = gi * NM + dm
                    s.wait_ge(ymm, slot + 1)
                    if slot >= 2:
                        s.wait_ge(ydv, slot - 1)
                    s.activation(yt[slot % 2][:, :], pY[slot % 2][:, :], AF.Identity,
                                 bias=b2_sb[e % 2][:, dm:dm + 1]).then_inc(yae, 1)

    @block.vector
    def _(v):
        for e in range(NE_RUN):
            for tc in range(NT):
                gi = e * NT + tc
                for dm in range(NM):
                    slot = gi * NM + dm
                    v.wait_ge(yae, slot + 1)
                    if slot >= 4:
                        v.wait_ge(qg, 16 * QG_ACC[slot - 4])
                    v.tensor_mul(ysc[slot % 4][:, :], yt[slot % 2][:, :],
                                 wbc[:, tc * TC:tc * TC + TC]).then_inc(ydv, 1)

    estk.close()
    _exp_scope.__exit__(None, None, None)


    @block.sync
    def _(sy):
        sy.wait_ge(qg, 16 * QG_TOT)
        sy.wait_ge(qs, 16 * QS0)

    emit_ln_sum(ctx, block, "p4ln", x1f, facc, x2f, x2b)





def emit_p5(ctx, block):
    T, TC, NT = ctx.T, ctx.TC, ctx.NT
    NKC = T // 128
    catT = ctx.inp("catT"); cab = ctx.inp("cab_c")
    q2lT = ctx.inp("q2lT_r"); q2lb = ctx.inp("q2lb_r")
    kv2lT = ctx.inp("kv2lT_r"); kv2lb = ctx.inp("kv2lb_r")
    wqkT = ctx.inp("wqkT_r"); wqkb = ctx.inp("wqkb_r")
    cosr = ctx.inp("cos_r"); sinr = ctx.inp("sin_r"); rT = ctx.inp("rT_r")
    ident = ctx.inp("ident")
    x2b = ctx.park("x2b")
    qkvp = ctx.park("qkvp")
    qfkp = ctx.park("qfkp"); krotp = ctx.park("krotp"); vaugp = ctx.park("vaugp")

    qs = ctx.sem("p5_qs"); qg = ctx.sem("p5_qg")

    astk = ExitStack()
    cat_sb = astk.enter_context(ctx.nc.sbuf_tensor("p5_cat", [128, NM * 3072], BF16))
    x2t = astk.enter_context(ctx.nc.sbuf_tensor("p5_x2t", [128, NM * T], BF16))
    cab_sb = astk.enter_context(ctx.nc.sbuf_tensor("p5_cab", [128, 24], F32))
    stg = [astk.enter_context(ctx.nc.sbuf_tensor(f"p5_st{i}", [128, TC], BF16)) for i in range(4)]
    pQ = [astk.enter_context(ctx.nc.psum_tensor(f"p5_pQ{i}", [128, TC], F32)) for i in range(2)]
    amm = ctx.sem("p5_amm"); aev = ctx.sem("p5_aev")

    @block.sync
    def _(sy):
        sy.dma_start(out=cab_sb[:, :], in_=cab).then_inc(qs, 16)

    @block.gpsimd
    def _(g):
        for k in range(NM):
            g.dma_start(out=cat_sb[:, k * 3072:(k + 1) * 3072],
                        in_=catT[k * 128:(k + 1) * 128, :]).then_inc(qg, 16)
        for k in range(NM):
            g.dma_start(out=x2t[:, k * T:(k + 1) * T],
                        in_=x2b[k * 128:(k + 1) * 128, :]).then_inc(qg, 16)

    @block.tensor
    def _(t):
        t.wait_ge(qg, 16 * 2 * NM)
        for m in range(24):
            for tc in range(NT):
                gi = m * NT + tc
                if gi >= 2:
                    t.wait_ge(aev, gi - 1)
                for k in range(NM):
                    ins = t.matmul(pQ[gi % 2][:, :],
                                   cat_sb[:, k * 3072 + 128 * m:k * 3072 + 128 * m + 128],
                                   x2t[:, k * T + tc * TC:k * T + tc * TC + TC],
                                   start=(k == 0), stop=(k == NM - 1))
                ins.then_inc(amm, 1)

    @block.scalar
    def _(s):
        s.wait_ge(qs, 16)
        for m in range(24):
            for tc in range(NT):
                gi = m * NT + tc
                s.wait_ge(amm, gi + 1)
                if gi >= 4:
                    s.wait_ge(qs, 16 * (1 + gi - 3))
                s.activation(stg[gi % 4][:, :], pQ[gi % 2][:, :], AF.Identity,
                             bias=cab_sb[:, m:m + 1]).then_inc(aev, 1)

    @block.sync
    def _(sy):
        for m in range(24):
            for tc in range(NT):
                gi = m * NT + tc
                sy.wait_ge(aev, gi + 1)
                sy.dma_start(out=qkvp[m * 128:(m + 1) * 128, tc * TC:tc * TC + TC],
                             in_=stg[gi % 4][:, :]).then_inc(qs, 16)

    astk.close()
    QSA = 1 + 24 * NT

    bstk = ExitStack()
    sbb = lambda n, sh, dt: bstk.enter_context(ctx.nc.sbuf_tensor(n, sh, dt))
    psb = lambda n, sh, dt=F32: bstk.enter_context(ctx.nc.psum_tensor(n, sh, dt))
    q2l_sb = sbb("p5_q2l", [64, 32], BF16)
    kv2l_sb = sbb("p5_kv2l", [64, 32], BF16)
    wqk_sb = sbb("p5_wqk", [32, 32], BF16)
    q2lb_sb = sbb("p5_q2lb", [32, 1], F32)
    kv2lb_sb = sbb("p5_kv2lb", [32, 1], F32)
    wqkb_sb = sbb("p5_wqkb", [32, 1], F32)
    cos_sb = sbb("p5_cos", [32, T], BF16)
    sin_sb = sbb("p5_sin", [32, T], BF16)
    rT_sb = sbb("p5_rT", [32, 32], BF16)
    id_sb = sbb("p5_id", [32, 32], BF16)
    qh = [sbb(f"p5_qh{i}", [64, T], BF16) for i in range(2)]
    kh = [sbb(f"p5_kh{i}", [64, T], BF16) for i in range(2)]
    vh = [sbb(f"p5_vh{i}", [64, T], BF16) for i in range(2)]
    qlat = sbb("p5_qlat", [32, T], BF16)
    klat = sbb("p5_klat", [32, T], BF16)
    vlat = sbb("p5_vlat", [32, T], BF16)
    rota = sbb("p5_rota", [32, T], F32)
    rotb = sbb("p5_rotb", [32, T], BF16)
    qrot = sbb("p5_qrot", [32, T], BF16)
    krot = [sbb(f"p5_krot{i}", [32, T], BF16) for i in range(2)]
    qfk = [sbb(f"p5_qfk{i}", [32, T], BF16) for i in range(2)]
    vaug = [sbb(f"p5_va{i}", [128, 33 * NKC], BF16) for i in range(2)]
    pA = [psb(f"p5_pA{i}", [32, TC]) for i in range(3)]
    pB = [psb(f"p5_pB{i}", [32, TC]) for i in range(2)]
    pV = [psb(f"p5_pV{i}", [128, 32], BF16) for i in range(2)]
    pVd = psb("p5_pVd", [128, 32], BF16)

    am = ctx.sem("p5_am"); av = ctx.sem("p5_av")
    bm = ctx.sem("p5_bm"); bv = ctx.sem("p5_bv")
    tm = ctx.sem("p5_tm"); tv = ctx.sem("p5_tv")

    @block.gpsimd
    def _(g):
        g.wait_ge(qs, 16 * QSA)
        for ap, tile in ((q2lT, q2l_sb), (kv2lT, kv2l_sb), (cosr, cos_sb),
                         (sinr, sin_sb)):
            g.dma_start(out=tile[:, :], in_=ap[0:tile.shape[0], :]).then_inc(qg, 16)
        g.dma_start(out=rT_sb[:, :], in_=rT[0:32, :]).then_inc(qg, 16)
        g.dma_start(out=wqk_sb[:, :], in_=wqkT[0:32, :]).then_inc(qg, 16)
        g.dma_start(out=id_sb[:, :], in_=ident[0:32, 0:32]).then_inc(qg, 16)
        for h in range(NH):
            if h < 2:
                g.wait_ge(qs, 16 * QSA)
            else:
                g.wait_ge(am, 4 * NT * (h - 1))
            g.dma_start(out=qh[h % 2][:, :], in_=qkvp[64 * h:64 * h + 64, :]).then_inc(qg, 16)
            g.dma_start(out=kh[h % 2][:, :], in_=qkvp[1024 + 64 * h:1024 + 64 * h + 64, :]).then_inc(qg, 16)
            g.dma_start(out=vh[h % 2][:, :], in_=qkvp[2048 + 64 * h:2048 + 64 * h + 64, :]).then_inc(qg, 16)

    @block.sync
    def _(sy):
        for ap, tile in ((q2lb, q2lb_sb), (kv2lb, kv2lb_sb), (wqkb, wqkb_sb)):
            sy.dma_start(out=tile[:, :], in_=ap[0:32, :]).then_inc(qs, 16)

    QGB = 2 * NM + 7
    QSB = QSA + 3

    @block.tensor
    def _(t):
        t.wait_ge(qs, 16 * QSB)
        for h in range(NH):
            t.wait_ge(qg, 16 * (QGB + 3 * (h + 1)))
            for cls in range(3):
                src_t = (qh, kh, vh)[cls][h % 2]
                lT = q2l_sb if cls == 0 else kv2l_sb
                for tc in range(NT):
                    li = (4 * h + cls) * NT + tc
                    if li >= 3:
                        t.wait_ge(av, li - 2)
                    t.matmul(pA[li % 3][:, :], lT[:, :],
                             src_t[:, tc * TC:tc * TC + TC],
                             start=True, stop=True).then_inc(am, 1)
            for cls in range(2):
                lat = qlat if cls == 0 else klat
                for tc in range(NT):
                    bi = (2 * h + cls) * NT + tc
                    t.wait_ge(av, (4 * h + cls) * NT + tc + 1)
                    if bi >= 2:
                        t.wait_ge(bv, bi - 1)
                    t.matmul(pB[bi % 2][:, :], rT_sb[:, :],
                             lat[:, tc * TC:tc * TC + TC],
                             start=True, stop=True).then_inc(bm, 1)
            t.wait_ge(bv, (2 * h + 1) * NT)
            for tc in range(NT):
                li = (4 * h + 3) * NT + tc
                if li >= 3:
                    t.wait_ge(av, li - 2)
                t.matmul(pA[li % 3][:, :], wqk_sb[:, :],
                         qrot[:, tc * TC:tc * TC + TC],
                         start=True, stop=True).then_inc(am, 1)
            t.wait_ge(av, (4 * h + 3) * NT)
            for kc in range(NKC):
                vi = h * NKC + kc
                if vi >= 2:
                    t.wait_ge(tv, vi - 1)
                t.transpose(pV[vi % 2][:, :], vlat[:, kc * 128:kc * 128 + 128],
                            id_sb[:, :]).then_inc(tm, 1)
            t.transpose(pVd[:, :], vlat[:, 0:128], id_sb[:, :]).then_inc(tm, 1)

    @block.scalar
    def _(s):
        for h in range(NH):
            for cls in range(3):
                dst = (qlat, klat, vlat)[cls]
                bcol = q2lb_sb if cls == 0 else kv2lb_sb
                if cls == 0:
                    s.wait_ge(bm, (2 * h - 1) * NT if h >= 1 else 0)
                if cls == 1:
                    s.wait_ge(bm, 2 * h * NT)
                if cls == 2:
                    s.wait_ge(tm, (NKC + 1) * h)
                for tc in range(NT):
                    li = (4 * h + cls) * NT + tc
                    s.wait_ge(am, li + 1)
                    s.activation(dst[:, tc * TC:tc * TC + TC], pA[li % 3][:, :],
                                 AF.Identity, bias=bcol[:, 0:1]).then_inc(av, 1)
            if h >= 2:
                s.wait_ge(qs, 16 * (QSB + 3 * (h - 1)))
            for tc in range(NT):
                li = (4 * h + 3) * NT + tc
                s.wait_ge(am, li + 1)
                s.activation(qfk[h % 2][:, tc * TC:tc * TC + TC], pA[li % 3][:, :],
                             AF.Identity, bias=wqkb_sb[:, 0:1]).then_inc(av, 1)

    @block.vector
    def _(v):
        for h in range(NH):
            v.wait_ge(am, 4 * NT * h)
            for tc in range(NT):
                bi = (2 * h) * NT + tc
                sl = slice(tc * TC, tc * TC + TC)
                v.wait_ge(bm, bi + 1)
                v.wait_ge(av, (4 * h) * NT + tc + 1)
                v.tensor_mul(rota[:, sl], pB[bi % 2][:, :], sin_sb[:, sl])
                v.tensor_mul(rotb[:, sl], qlat[:, sl], cos_sb[:, sl])
                v.tensor_add(qrot[:, sl], rota[:, sl], rotb[:, sl]).then_inc(bv, 1)
            if h >= 2:
                v.wait_ge(qs, 16 * (QSB + 3 * (h - 1)))
            for tc in range(NT):
                bi = (2 * h + 1) * NT + tc
                sl = slice(tc * TC, tc * TC + TC)
                v.wait_ge(bm, bi + 1)
                v.wait_ge(av, (4 * h + 1) * NT + tc + 1)
                v.tensor_mul(rota[:, sl], pB[bi % 2][:, :], sin_sb[:, sl])
                v.tensor_mul(rotb[:, sl], klat[:, sl], cos_sb[:, sl])
                v.tensor_add(krot[h % 2][:, sl], rota[:, sl], rotb[:, sl]).then_inc(bv, 1)
            if h >= 2:
                v.wait_ge(qs, 16 * (QSB + 3 * (h - 1)))
            v.memset(vaug[h % 2][:, :], 1.0)
            for kc in range(NKC):
                vi = h * NKC + kc
                v.wait_ge(tm, h * (NKC + 1) + kc + 2)
                v.tensor_copy(vaug[h % 2][:, kc * 33:kc * 33 + 32],
                              pV[vi % 2][:, :]).then_inc(tv, 1)

    @block.sync
    def _(sy):
        for h in range(NH):
            gI = h // 4; b = h % 4
            sy.wait_ge(av, 4 * NT * (h + 1))
            sy.dma_start(out=qfkp[gI * 128 + 32 * b:gI * 128 + 32 * b + 32, :],
                         in_=qfk[h % 2][:, :]).then_inc(qs, 16)
            sy.wait_ge(bv, (2 * h + 2) * NT)
            sy.dma_start(out=krotp[gI * 128 + 32 * b:gI * 128 + 32 * b + 32, :],
                         in_=krot[h % 2][:, :]).then_inc(qs, 16)
            sy.wait_ge(tv, NKC * (h + 1))
            sy.dma_start(out=vaugp[:, h * 33 * NKC:(h + 1) * 33 * NKC],
                         in_=vaug[h % 2][:, :]).then_inc(qs, 16)
        sy.wait_ge(qs, 16 * (QSB + 3 * NH))

    if DBG_GATE:
        dbg5 = {}
        for n, tile in (("qlat", qlat), ("klat", klat), ("vlat", vlat),
                        ("qrot", qrot), ("krot5", krot[1]), ("qfk5", qfk[1])):
            dbg5[n] = (ctx.nc.dram_tensor("dbg5_" + n, list(tile.shape), BF16,
                                          kind="ExternalOutput").ap(), tile)

        @block.sync
        def _(sy):
            sy.wait_ge(ctx.sem("p5_tv"), NKC * NH)
            for n, (ap, tile) in dbg5.items():
                sy.dma_start(out=ap, in_=tile[:, :]).then_inc(ctx.sem("p5_qdbg"), 16)
            sy.wait_ge(ctx.sem("p5_qdbg"), 16 * len(dbg5))


    bstk.close()



def emit_p6(ctx, block):
    T, TC, NT = ctx.T, ctx.TC, ctx.NT
    NKC = T // 128
    NR = TC // 128
    mask = ctx.inp("mask01"); onr = ctx.inp("ones_row")
    krotp = ctx.park("krotp"); qfkp = ctx.park("qfkp"); vaugp = ctx.park("vaugp")
    cphp = ctx.park("cphp"); cbias = ctx.park("cbias")
    x2f = ctx.park("x2f"); a1p = ctx.park("a1p")
    x3f = ctx.park("x3f"); x3b = ctx.park("x3b")

    qs = ctx.sem("p6_qs"); qg = ctx.sem("p6_qg")

    mstk = ExitStack()
    sbm = lambda n, sh, dt: mstk.enter_context(ctx.nc.sbuf_tensor(n, sh, dt))
    psm = lambda n, sh, dt=F32: mstk.enter_context(ctx.nc.psum_tensor(n, sh, dt))

    kro = sbm("p6_kro", [128, 4 * T], BF16)
    vau = sbm("p6_vau", [128, NH * 33 * NKC], BF16)
    cph = sbm("p6_cph", [128, 4 * 1024], BF16)
    cb_sb = sbm("p6_cb", [128, 8], F32)
    msk = sbm("p6_msk", [128, NR * TC], BF16)
    onr_sb = sbm("p6_onr", [1, 512], BF16)
    qf = [sbm(f"p6_qf{i}", [128, 4 * TC], BF16) for i in range(2)]
    att = [sbm(f"p6_att{i}", [128, TC], BF16) for i in range(4)]
    ysb = sbm("p6_ysb", [32, TC], F32)
    rr0 = sbm("p6_rr0", [1, TC], F32)
    rrb = sbm("p6_rrb", [1, TC], BF16)
    ynp = [sbm(f"p6_ynp{i}", [128, TC], BF16) for i in range(4)]
    a1stg = [sbm(f"p6_a1s{i}", [128, TC], F32) for i in range(4)]
    pS = [psm(f"p6_pS{i}", [128, TC]) for i in range(4)]
    pY = psm("p6_pY", [33, TC])
    pBC = psm("p6_pBC", [32, TC])
    pC = [psm(f"p6_pC{i}", [128, TC]) for i in range(2)]

    qmm = ctx.sem("p6_qmm"); aex = ctx.sem("p6_aex"); dmk = ctx.sem("p6_dmk")
    avm = ctx.sem("p6_avm"); yev = ctx.sem("p6_yev"); rbv = ctx.sem("p6_rbv")
    bcm = ctx.sem("p6_bcm"); ydn = ctx.sem("p6_ydn")
    cpm = ctx.sem("p6_cpm"); cv6 = ctx.sem("p6_cv6"); a1n = ctx.sem("p6_a1n")

    def kcmax(qc):
        return (qc + 1) * NR
    CT = [0]
    for qc in range(NT):
        CT.append(CT[-1] + NH * kcmax(qc))
    def sidx(qc, h, kc):
        return CT[qc] + h * kcmax(qc) + kc
    def is_diag(qc, kc):
        return kc >= qc * NR
    DIAG = []
    dcnt = 0
    for qc in range(NT):
        for h in range(NH):
            for kc in range(kcmax(qc)):
                if is_diag(qc, kc):
                    dcnt += 1
                DIAG.append(dcnt)

    @block.gpsimd
    def _(g):
        for gI in range(4):
            g.dma_start(out=kro[:, gI * T:(gI + 1) * T],
                        in_=krotp[gI * 128:(gI + 1) * 128, :]).then_inc(qg, 16)
            g.dma_start(out=cph[:, gI * 1024:(gI + 1) * 1024],
                        in_=cphp[gI * 128:(gI + 1) * 128, :]).then_inc(qg, 16)
        g.dma_start(out=vau[:, :], in_=vaugp).then_inc(qg, 16)
        g.dma_start(out=msk[:, :], in_=mask).then_inc(qg, 16)
        g.dma_start(out=onr_sb[:, :], in_=onr).then_inc(qg, 16)

    STORE_CNT = []
    _c = 1
    for qc in range(NT):
        _c += 4
        if qc >= 1:
            for m in range(NM):
                _c += 1
                STORE_CNT.append(_c)
    for m in range(NM):
        _c += 1
        STORE_CNT.append(_c)
    QS_TOT6 = _c

    @block.sync
    def _(sy):
        sy.dma_start(out=cb_sb[:, :], in_=cbias).then_inc(qs, 16)
        for qc in range(NT):
            if qc >= 2:
                sy.wait_ge(qmm, CT[qc - 1])
            for gI in range(4):
                sy.dma_start(out=qf[qc % 2][:, gI * TC:(gI + 1) * TC],
                             in_=qfkp[gI * 128:(gI + 1) * 128, qc * TC:qc * TC + TC]).then_inc(qs, 16)
            if qc >= 1 and P6_TAIL:
                for m in range(NM):
                    idx = (qc - 1) * NM + m
                    sy.wait_ge(a1n, idx + 1)
                    sy.dma_start(out=a1p[m * 128:(m + 1) * 128, (qc - 1) * TC:(qc - 1) * TC + TC],
                                 in_=a1stg[idx % 4][:, :]).then_inc(qs, 16)
        if P6_TAIL:
            for m in range(NM):
                idx = (NT - 1) * NM + m
                sy.wait_ge(a1n, idx + 1)
                sy.dma_start(out=a1p[m * 128:(m + 1) * 128, (NT - 1) * TC:(NT - 1) * TC + TC],
                             in_=a1stg[idx % 4][:, :]).then_inc(qs, 16)
            sy.wait_ge(qs, 16 * QS_TOT6)
        else:
            sy.wait_ge(ctx.sem("p6_ydn"), P6_NH * NT)
            for gg in range(4):
                ap = ctx.nc.dram_tensor(f"dbg6_ynp{gg}", [128, TC], BF16,
                                        kind="ExternalOutput").ap()
                sy.dma_start(out=ap, in_=ynp[gg][:, :]).then_inc(qs, 16)
            sy.wait_ge(qs, 16 * (1 + 4 * NT + 4))

    @block.tensor
    def _(t):
        t.wait_ge(qg, 16 * 11)
        t.wait_ge(qs, 16)
        for qc in range(NT):
            t.wait_ge(qs, 16 * (1 + 4 * (qc + 1)))
            for h in range(P6_NH):
                gI = h // 4; b = h % 4
                km = kcmax(qc)
                for kc in range(km + 2):
                    if kc < km:
                        si = sidx(qc, h, kc)
                        if si >= 4:
                            t.wait_ge(aex, si - 3)
                        t.matmul(pS[si % 4][:, :],
                                 kro[32 * b:32 * b + 32, gI * T + kc * 128:gI * T + kc * 128 + 128],
                                 qf[qc % 2][32 * b:32 * b + 32, gI * TC:(gI + 1) * TC],
                                 start=True, stop=True,
                                 tile_position=(32 * b, 0)).then_inc(qmm, 1)
                    if kc == 1 and h >= 1:
                        t.wait_ge(rbv, qc * NH + h)
                        t.matmul(pBC[:, :], onr_sb[0:1, 0:32], rrb[:, :],
                                 start=True, stop=True).then_inc(bcm, 1)
                    if kc >= 2:
                        kd = kc - 2
                        sd = sidx(qc, h, kd)
                        if kd == 0:
                            t.wait_ge(rbv, qc * NH + h)
                        t.wait_ge(dmk, DIAG[sd])
                        t.wait_ge(aex, sd + 1)
                        t.matmul(pY[:, :], vau[:, (h * NKC + kd) * 33:(h * NKC + kd) * 33 + 33],
                                 att[sd % 4][:, :],
                                 start=(kd == 0), stop=(kd == km - 1)).then_inc(avm, 1)
            t.wait_ge(rbv, qc * NH + P6_NH)
            t.matmul(pBC[:, :], onr_sb[0:1, 0:32], rrb[:, :],
                     start=True, stop=True).then_inc(bcm, 1)
            if not P6_TAIL:
                continue
            t.wait_ge(ydn, NH * (qc + 1))
            for m in range(NM):
                ci = qc * NM + m
                if ci >= 2:
                    t.wait_ge(a1n, ci - 1)
                for gI in range(4):
                    ins = t.matmul(pC[ci % 2][:, :],
                                   cph[:, gI * 1024 + 128 * m:gI * 1024 + 128 * m + 128],
                                   ynp[gI][:, :],
                                   start=(gI == 0), stop=(gI == 3))
                ins.then_inc(cpm, 1)

    @block.scalar
    def _(s):
        isc = 1.0 / float(np.sqrt(LAT))
        for qc in range(NT):
            for h in range(P6_NH):
                for kc in range(kcmax(qc)):
                    si = sidx(qc, h, kc)
                    s.wait_ge(qmm, si + 1)
                    if si >= 4:
                        s.wait_ge(avm, si - 3)
                    s.activation(att[si % 4][:, :], pS[si % 4][:, :], AF.Exp,
                                 scale=isc).then_inc(aex, 1)
                hi = qc * NH + h
                s.wait_ge(avm, sidx(qc, h, kcmax(qc) - 1) + 1)
                s.activation(ysb[:, :], pY[0:32, :], AF.Identity).then_inc(yev, 1)
            if not P6_TAIL:
                continue
            for m in range(NM):
                ci = qc * NM + m
                s.wait_ge(cpm, ci + 1)
                if ci >= 4:
                    s.wait_ge(qs, 16 * STORE_CNT[ci - 4])
                s.activation(a1stg[ci % 4][:, :], pC[ci % 2][:, :], AF.Identity,
                             bias=cb_sb[:, m:m + 1]).then_inc(a1n, 1)

    @block.vector
    def _(v):
        for qc in range(NT):
            for h in range(P6_NH):
                gI = h // 4; b = h % 4
                for kc in range(kcmax(qc)):
                    si = sidx(qc, h, kc)
                    if is_diag(qc, kc):
                        r = kc - qc * NR
                        v.wait_ge(aex, si + 1)
                        v.tensor_mul(att[si % 4][:, :], att[si % 4][:, :],
                                     msk[:, r * TC:(r + 1) * TC]).then_inc(dmk, 1)
                hi = qc * NH + h
                v.wait_ge(yev, hi + 1)
                v.reciprocal(rr0[:, :], pY[32:33, :])
                v.tensor_copy(rrb[:, :], rr0[:, :]).then_inc(rbv, 1)
                v.wait_ge(bcm, hi + 1)
                if qc >= 1 and h < 4:
                    v.wait_ge(cpm, NM * qc)
                v.tensor_mul(ynp[gI][32 * b:32 * b + 32, :], ysb[:, :],
                             pBC[:, :]).then_inc(ydn, 1)

    mstk.close()

    if P6_TAIL:
        emit_ln_sum(ctx, block, "p6ln", x2f, a1p, x3f, x3b)



def emit_p7(ctx, block):
    T, TC, NT = ctx.T, ctx.TC, ctx.NT
    w0 = ctx.inp("l1w0"); w1a = ctx.inp("l1w1"); w2a = ctx.inp("l1w2")
    lb = ctx.inp("l1b_c")
    x3f = ctx.park("x3f"); x3b = ctx.park("x3b"); f1p = ctx.park("f1p")
    outf = ctx.out_park()

    qs = ctx.sem("p7_qs"); qg = ctx.sem("p7_qg")
    mstk = ExitStack()
    sbm = lambda n, sh, dt: mstk.enter_context(ctx.nc.sbuf_tensor(n, sh, dt))
    psm = lambda n, sh: mstk.enter_context(ctx.nc.psum_tensor(n, sh, F32))
    w_sb = [sbm(f"p7_w{i}", [128, NM * 1024], BF16) for i in range(3)]
    lb_sb = sbm("p7_lb", [128, 8], F32)
    x3t = [sbm(f"p7_x3{i}", [128, NM * TC], BF16) for i in range(2)]
    h1 = sbm("p7_h1", [128, NM * TC], BF16)
    h2 = sbm("p7_h2", [128, NM * TC], BF16)
    f1s = [sbm(f"p7_f1{i}", [128, TC], F32) for i in range(4)]
    pP = [psm(f"p7_p{i}", [128, TC]) for i in range(2)]

    mm = ctx.sem("p7_mm"); ev = ctx.sem("p7_ev"); fn = ctx.sem("p7_fn")

    @block.gpsimd
    def _(g):
        for i, w in enumerate((w0, w1a, w2a)):
            for k in range(NM):
                g.dma_start(out=w_sb[i][:, k * 1024:(k + 1) * 1024],
                            in_=w[k * 128:(k + 1) * 128, :]).then_inc(qg, 16)

    STORE_CNT7 = []
    _c = 1
    for tc in range(NT):
        _c += NM
        if tc >= 1:
            for m in range(NM):
                _c += 1
                STORE_CNT7.append(_c)
    for m in range(NM):
        _c += 1
        STORE_CNT7.append(_c)
    QS_TOT7 = _c

    @block.sync
    def _(sy):
        sy.dma_start(out=lb_sb[:, :], in_=lb).then_inc(qs, 16)
        for tc in range(NT):
            if tc >= 2:
                sy.wait_ge(mm, 3 * NM * (tc - 1))
            for m in range(NM):
                sy.dma_start(out=x3t[tc % 2][:, m * TC:(m + 1) * TC],
                             in_=x3b[m * 128:(m + 1) * 128, tc * TC:tc * TC + TC]).then_inc(qs, 16)
            if tc >= 1:
                for m in range(NM):
                    idx = (tc - 1) * NM + m
                    sy.wait_ge(fn, idx + 1)
                    sy.dma_start(out=f1p[m * 128:(m + 1) * 128, (tc - 1) * TC:(tc - 1) * TC + TC],
                                 in_=f1s[idx % 4][:, :]).then_inc(qs, 16)
        for m in range(NM):
            idx = (NT - 1) * NM + m
            sy.wait_ge(fn, idx + 1)
            sy.dma_start(out=f1p[m * 128:(m + 1) * 128, (NT - 1) * TC:(NT - 1) * TC + TC],
                         in_=f1s[idx % 4][:, :]).then_inc(qs, 16)
        sy.wait_ge(qs, 16 * QS_TOT7)

    @block.tensor
    def _(t):
        t.wait_ge(qg, 16 * 3 * NM)
        for tc in range(NT):
            t.wait_ge(qs, 16 * (1 + NM * (tc + 1)))
            for stage in range(3):
                src = (x3t[tc % 2], h1, h2)[stage]
                for m in range(NM):
                    gi = (tc * 3 + stage) * NM + m
                    if gi >= 2:
                        prev = gi - 2
                        tp = prev // (3 * NM); sp = (prev // NM) % 3; mp = prev % NM
                        if sp == 2:
                            t.wait_ge(fn, NM * tp + mp + 1)
                        else:
                            t.wait_ge(ev, 2 * NM * tp + NM * sp + mp + 1)
                    if stage > 0:
                        t.wait_ge(ev, 2 * NM * tc + NM * stage)
                    for k in range(NM):
                        ins = t.matmul(pP[gi % 2][:, :],
                                       w_sb[stage][:, k * 1024 + 128 * m:k * 1024 + 128 * m + 128],
                                       src[:, k * TC:(k + 1) * TC],
                                       start=(k == 0), stop=(k == NM - 1))
                    ins.then_inc(mm, 1)

    @block.scalar
    def _(s):
        s.wait_ge(qs, 16)
        for tc in range(NT):
            for stage in range(3):
                dst = (h1, h2, None)[stage]
                for m in range(NM):
                    gi = (tc * 3 + stage) * NM + m
                    s.wait_ge(mm, gi + 1)
                    if stage < 2:
                        if tc >= 1:
                            s.wait_ge(mm, ((tc - 1) * 3 + stage + 1) * NM + NM)
                        s.activation(dst[:, m * TC:(m + 1) * TC], pP[gi % 2][:, :],
                                     AF.Identity).then_inc(ev, 1)
                    else:
                        idx = tc * NM + m
                        if idx >= 4:
                            s.wait_ge(qs, 16 * STORE_CNT7[idx - 4])
                        s.activation(f1s[idx % 4][:, :], pP[gi % 2][:, :], AF.Tanh,
                                     bias=lb_sb[:, m:m + 1]).then_inc(fn, 1)

    mstk.close()

    emit_ln_sum(ctx, block, "p7ln", x3f, f1p, outf, None)



EMITTERS = {"p0": emit_p0, "p1": emit_p1, "p2": emit_p2, "p3": emit_p3, "p4": emit_p4, "p5": emit_p5, "p6": emit_p6, "p7": emit_p7}


def build(T, phases, prep_shapes):
    nc = bass.Bass("TRN2", target_bir_lowering=False)
    stack = ExitStack()
    ctx = Ctx(nc, T, stack, phases, prep_shapes)
    with nc.Block() as block:
        for i, ph in enumerate(phases):
            ctx.begin_phase()
            with nc.named_scope(ph):
                EMITTERS[ph](ctx, block)
            ctx.end_phase()
            if i < len(phases) - 1:
                barrier(ctx, block, ph + "a")
                sems = ctx.phase_semlist
                ctx.phase_semlist = []
                keep = {k: v for k, v in ctx.sems.items()
                        if k == "bar" or "_q" in k}
                ctx.sems = keep
                if sems:
                    def clr(g, sems=sems):
                        for s in sems:
                            g.sem_clear(s)
                    block.gpsimd(clr)
                    barrier(ctx, block, ph + "b")
                    for s in sems:
                        nc.release_semaphore(s)
    stack.close()
    return nc, ctx


def make_prep_shapes(prep, T):
    shapes = {k: (v.shape, v.dtype.type if hasattr(v.dtype, 'type') else v.dtype)
              for k, v in prep.items()}
    shapes = {k: (v.shape, (BF if v.dtype == BF else np.float32)) for k, v in prep.items()}
    shapes["xf"] = ((DM, T), np.float32)
    return shapes






ALL_PHASES = ["p0", "p1", "p2", "p3", "p4", "p5", "p6", "p7"]


def kernel(**inputs):
    x = np.asarray(inputs["L0_attn_x"])
    T = int(x.shape[1])
    prep, xfs = host_prep(inputs, T)
    shapes = make_prep_shapes(prep, T)
    nc, ctx = build(T, ALL_PHASES, shapes)
    names = list(ctx.inputs.keys())
    in_maps = []
    for c in range(x.shape[0]):
        m = {}
        for n in names:
            m["i_" + n] = xfs[c] if n == "xf" else prep[n]
        in_maps.append(m)
    res = run_bass_kernel_spmd(nc, in_maps, core_ids=list(range(len(in_maps))))
    out = np.stack([np.ascontiguousarray(np.asarray(r["outf"]).T) for r in res.results])
    return out.astype(np.float32)


# version 14
# speedup vs baseline: 1.0740x; 1.0740x over previous
import numpy as np
import ml_dtypes
from contextlib import ExitStack

import concourse.bass as bass
import concourse.mybir as mybir
from concourse.bass_utils import run_bass_kernel_spmd

F32 = mybir.dt.float32
BF16 = mybir.dt.bfloat16
AF = mybir.ActivationFunctionType
ALU = mybir.AluOpType
BF = ml_dtypes.bfloat16

DM = 1024
DI = 2048
DS = 16
DCONV = 4
DTR = 32
NE = 8
FF = 4096
NH = 16
HD = 64
LAT = 32
NDB = DI // 128
NM = DM // 128

NEG = -1.0e30
NE_RUN = NE
DBG_GATE = False
P6_NH = NH
P6_TAIL = True



def _cols(b):
    b = np.asarray(b, np.float32)
    return np.ascontiguousarray(b.reshape(-1, 128).T)


def _db_regroup(a):
    a = np.asarray(a, np.float32)
    k = a.shape[1]
    return np.ascontiguousarray(a.reshape(NDB, 128, k).transpose(1, 0, 2).reshape(128, NDB * k))


def _rope_tables(T):
    inv_freq = 1.0 / 10000.0 ** (np.arange(0, LAT, 2, dtype=np.float64) / LAT)
    t = np.arange(T, dtype=np.float64)
    freqs = np.outer(t, inv_freq)
    cos = np.repeat(np.cos(freqs), 2, axis=-1)[:, :LAT]
    sin = np.repeat(np.sin(freqs), 2, axis=-1)[:, :LAT]
    return cos, sin


def host_prep(inputs, T):
    d = {}
    g = lambda n: np.asarray(inputs[n], np.float32)

    x = g("L0_attn_x")
    xfs = [np.ascontiguousarray(x[c].T) for c in range(x.shape[0])]

    d["wiT"] = np.ascontiguousarray(g("L0_attn_Wi").T)
    d["bi_c"] = _cols(g("L0_attn_bi"))
    d["convw_c"] = _db_regroup(g("L0_attn_conv_w")[:, 0, :])
    d["convb_c"] = _cols(g("L0_attn_conv_b"))
    d["wdtT"] = np.ascontiguousarray(g("L0_attn_Wdt").T)
    d["bdt_c"] = _cols(g("L0_attn_bdt"))
    d["alog_c"] = _db_regroup(g("L0_attn_A_log"))
    d["bm_c"] = _db_regroup(g("L0_attn_Bm"))
    d["cm_c"] = _db_regroup(g("L0_attn_Cm"))
    d["dv_c"] = _cols(g("L0_attn_Dv"))
    d["woT"] = np.ascontiguousarray(g("L0_attn_Wo").T)
    d["bo_c"] = _cols(g("L0_attn_bo"))

    d["wg"] = g("L0_ffn_Wg")
    d["bg_c"] = np.ascontiguousarray(g("L0_ffn_bg").reshape(8, 1))
    d["w1"] = g("L0_ffn_W1")
    d["b1_c"] = np.stack([_cols(b) for b in g("L0_ffn_b1")])
    d["w2"] = g("L0_ffn_W2")
    d["b2_c"] = np.stack([_cols(b) for b in g("L0_ffn_b2")])

    d["catT"] = np.ascontiguousarray(g("L1_attn_c_attn_w").T)
    d["cab_c"] = _cols(g("L1_attn_c_attn_b"))
    q2lT = np.ascontiguousarray(g("L1_attn_q2l_w").T)
    d["q2lT_r"] = np.concatenate([q2lT, q2lT], 0)
    d["q2lb_r"] = np.concatenate([g("L1_attn_q2l_b")] * 2).reshape(64, 1).astype(np.float32)
    kv2lT = np.ascontiguousarray(g("L1_attn_kv2l_w").T)
    d["kv2lT_r"] = np.concatenate([kv2lT, kv2lT], 0)
    d["kv2lb_r"] = np.concatenate([g("L1_attn_kv2l_b")] * 2).reshape(64, 1).astype(np.float32)
    wqkT = np.ascontiguousarray(g("L1_attn_wqk_w").T)
    d["wqkT_r"] = np.concatenate([wqkT, wqkT], 0)
    d["wqkb_r"] = np.concatenate([g("L1_attn_wqk_b")] * 2).reshape(64, 1).astype(np.float32)
    l2o = g("L1_attn_l2o_w")
    d["l2o_r"] = np.concatenate([l2o, l2o], 0)
    d["l2ob_t"] = _cols(np.tile(g("L1_attn_l2o_b"), NH))
    d["cprojT"] = np.ascontiguousarray(g("L1_attn_cproj_w").T)
    d["cpb_c"] = _cols(g("L1_attn_cproj_b"))

    d["l1w0"] = g("L1_ffn_weight")
    d["l1w1"] = g("L1_ffn_aux_weight1")
    d["l1w2"] = g("L1_ffn_aux_weight2")
    d["l1b_c"] = _cols(g("L1_ffn_bias"))

    d["ident"] = np.eye(128, dtype=BF)
    d["identf"] = np.eye(128, dtype=np.float32)
    d["ones_col"] = np.ones((128, 1), BF)
    d["ones_row"] = np.ones((1, 512), BF)
    d["ones_colf"] = np.ones((128, 1), np.float32)
    d["ones_rowf"] = np.ones((1, 128), np.float32)
    cos, sin = _rope_tables(T)
    d["cos_r"] = np.concatenate([cos.T, cos.T], 0).astype(BF)
    d["sin_r"] = np.concatenate([sin.T, sin.T], 0).astype(BF)
    R = np.zeros((LAT, LAT), np.float32)
    for i in range(LAT // 2):
        R[2 * i, 2 * i + 1] = -1.0
        R[2 * i + 1, 2 * i] = 1.0
    RT = np.ascontiguousarray(R.T)
    d["rT_r"] = np.concatenate([RT, RT], 0).astype(BF)
    TCW = min(512, T)
    NR = TCW // 128
    m = np.zeros((128, NR * TCW), BF)
    p = np.arange(128).reshape(128, 1)
    f = np.arange(TCW).reshape(1, TCW)
    for r in range(NR):
        m[:, r * TCW:(r + 1) * TCW] = (f >= 128 * r + p).astype(BF)
    d["mask01"] = m
    return d, xfs



def park_specs(T):
    NKC = T // 128
    return {
        "xinb": ([DI, T], BF16),
        "x32b": ([DTR, T], BF16),
        "szb": ([DI, T], BF16),
        "yzb": ([DI, T], BF16),
        "x1f": ([DM, T], F32),
        "x1b": ([DM, T], BF16),
        "wrow": ([NE, T], F32),
        "facc": ([DM, T], F32),
        "x2f": ([DM, T], F32),
        "x2b": ([DM, T], BF16),
        "qfkp": ([4 * 128, T], BF16),
        "krotp": ([4 * 128, T], BF16),
        "vaugp": ([128, NH * 33 * NKC], BF16),
        "cphp": ([4 * 128, DM], BF16),
        "cbias": ([128, NM], F32),
        "dvm1": ([128, 1], F32),
        "x3f": ([DM, T], F32),
        "x3b": ([DM, T], BF16),
        "a1p": ([DM, T], F32),
        "f1p": ([DM, T], F32),
        "qkvp": ([3 * DM, T], BF16),
    }


PARK_PROD = {
    "xinb": "p1", "x32b": "p1", "szb": "p1",
    "yzb": "p2",
    "x1f": "p3", "x1b": "p3",
    "wrow": "p4", "facc": "p4",
    "x2f": "p4", "x2b": "p4",
    "qfkp": "p5", "krotp": "p5", "vaugp": "p5",
    "cphp": "p0", "cbias": "p0", "dvm1": "p0",
    "x3f": "p6", "x3b": "p6",
    "a1p": "p6", "f1p": "p7",
    "qkvp": "p5",
}
PARK_CONS = {
    "xinb": ["p2"], "x32b": ["p2"], "szb": ["p2"],
    "yzb": ["p3"],
    "x1f": ["p4"], "x1b": ["p4"],
    "wrow": ["p4", "dbg"], "facc": ["p4", "dbg"],
    "x2f": ["p6"], "x2b": ["p5"],
    "qfkp": ["p6"], "krotp": ["p6"], "vaugp": ["p6"],
    "cphp": ["p6"], "cbias": ["p6"], "dvm1": ["p3"],
    "x3f": ["p7"], "x3b": ["p7"],
    "a1p": ["p6"], "f1p": ["p7"],
    "qkvp": ["p5", "dbg"],
}

PHASE_ORDER = ["p0", "p1", "p2", "p3", "p4", "p5", "p6", "p7"]

PHASE_INPUTS = {
    "p0": ["dv_c", "l2o_r", "cprojT", "cpb_c", "l2ob_t", "ident", "ones_col"],
    "p1": ["xf", "wiT", "bi_c"],
    "p2": ["wdtT", "bdt_c", "convw_c", "convb_c", "alog_c", "bm_c", "cm_c", "x32b_in" if False else "wdtT"],
    "p3": ["xf", "woT", "bo_c", "ones_row"],
    "p4": ["wg", "bg_c", "w1", "b1_c", "w2", "b2_c", "ident", "ones_col", "ones_row"],
    "p5": ["catT", "cab_c", "q2lT_r", "q2lb_r", "kv2lT_r", "kv2lb_r", "wqkT_r", "wqkb_r",
           "cos_r", "sin_r", "rT_r", "ident"],
    "p6": ["mask01", "ones_col", "ones_row"],
    "p7": ["l1w0", "l1w1", "l1w2", "l1b_c"],
}

INPUT_SHAPES = None



class Ctx:
    def __init__(self, nc, T, stack, phases, prep_shapes):
        self.nc = nc
        self.T = T
        self.TC = min(512, T)
        self.NT = T // self.TC
        self.stack = stack
        self.phases = phases
        self.prep_shapes = prep_shapes
        self.sems = {}
        self.phase_semlist = []
        self.inputs = {}
        self.parks = {}
        self.out = None

    def sem(self, name):
        if name not in self.sems:
            h = self.nc.alloc_semaphore(name=name)
            self.sems[name] = [h, 0]
            if name != "bar" and "_q" not in name:
                self.phase_semlist.append(h)
        return self.sems[name][0]

    def bump(self, name, k):
        self.sem(name)
        self.sems[name][1] += k
        return self.sems[name][1]

    def val(self, name):
        self.sem(name)
        return self.sems[name][1]

    def inp(self, name):
        if name not in self.inputs:
            shape, npdt = self.prep_shapes[name]
            dt = BF16 if npdt == BF else F32
            self.inputs[name] = self.nc.dram_tensor("i_" + name, list(shape), dt,
                                                    kind="ExternalInput").ap()
        return self.inputs[name]

    def park(self, name):
        if name not in self.parks:
            shape, dt = park_specs(self.T)[name]
            prod = PARK_PROD[name]
            cons = PARK_CONS[name]
            in_here = prod in self.phases
            used_here = any(c in self.phases for c in cons)
            used_later = any(c not in self.phases for c in cons)
            if in_here and not used_later:
                kind = "Internal"
            elif in_here:
                kind = "ExternalOutput"
            else:
                kind = "ExternalInput"
            self.parks[name] = (self.nc.dram_tensor("k_" + name, shape, dt, kind=kind).ap(), kind)
        return self.parks[name][0]

    def begin_phase(self):
        self.pstack = ExitStack()

    def end_phase(self):
        self.pstack.close()

    def out_park(self):
        if self.out is None:
            self.out = self.nc.dram_tensor("outf", [DM, self.T], F32,
                                           kind="ExternalOutput").ap()
        return self.out

    def sbuf(self, name, shape, dt):
        return self.pstack.enter_context(self.nc.sbuf_tensor(name, shape, dt))

    def psum(self, name, shape, dt=F32):
        return self.pstack.enter_context(self.nc.psum_tensor(name, shape, dt))


def barrier(ctx, block, tag):
    b = ctx.sem("bar")
    for engf in (block.sync, block.vector, block.scalar, block.tensor, block.gpsimd):
        def mk(engf):
            def f(eng):
                eng.sem_inc(b, 1)
            return f
        engf(mk(engf))
    tgt = ctx.bump("bar", 5)
    for engf in (block.sync, block.vector, block.scalar, block.tensor, block.gpsimd):
        def mk2(engf):
            def f(eng):
                eng.wait_ge(b, tgt)
            return f
        engf(mk2(engf))



def emit_p0(ctx, block):
    dv = ctx.inp("dv_c"); l2o = ctx.inp("l2o_r"); cpT = ctx.inp("cprojT")
    cpb = ctx.inp("cpb_c"); l2ob = ctx.inp("l2ob_t")
    onc = ctx.inp("ones_colf"); onr = ctx.inp("ones_rowf")
    dvm1 = ctx.park("dvm1"); cphp = ctx.park("cphp"); cbias = ctx.park("cbias")

    dv_sb = ctx.sbuf("p0_dv", [128, 16], F32)
    dvs = ctx.sbuf("p0_dvs", [128, 1], F32)
    onc_sb = ctx.sbuf("p0_onc", [128, 1], F32)
    onr_sb = ctx.sbuf("p0_onr", [1, 128], F32)
    sc_b = ctx.sbuf("p0_scb", [1, 1], F32)
    dvm_sb = ctx.sbuf("p0_dvm", [128, 1], F32)
    cpT_sb = ctx.sbuf("p0_cpT", [128, 8 * 1024], BF16)
    l2o_sb = ctx.sbuf("p0_l2o", [128, 32], BF16)
    l2ob_sb = ctx.sbuf("p0_l2ob", [128, 8], BF16)
    cpb_sb = ctx.sbuf("p0_cpb", [128, 8], F32)
    cph_sb = ctx.sbuf("p0_cph", [128, 4 * 1024], BF16)
    cbias_sb = ctx.sbuf("p0_cbias", [128, 8], F32)
    ps_a = ctx.psum("p0_psa", [128, 512], F32)
    ps_b = ctx.psum("p0_psb", [128, 512], F32)
    ps_1 = ctx.psum("p0_ps1", [1, 1], F32)
    ps_c = ctx.psum("p0_psc", [128, 8], F32)
    dmy0 = ctx.sbuf("p0_dmy", [1, 1], F32)

    qs = ctx.sem("p0_qs"); qg = ctx.sem("p0_qg")
    v0 = ctx.sem("p0_v0"); v1 = ctx.sem("p0_v1"); v2 = ctx.sem("p0_v2")
    t0 = ctx.sem("p0_t0"); t1 = ctx.sem("p0_t1"); t2 = ctx.sem("p0_t2")
    a0 = ctx.sem("p0_a0")

    @block.sync
    def _(sy):
        for ap, t in ((dv, dv_sb), (cpb, cpb_sb)):
            sy.dma_start(out=t[:, :], in_=ap).then_inc(qs, 16)

    @block.gpsimd
    def _(g):
        for ap, t in ((onc, onc_sb), (onr, onr_sb), (l2o, l2o_sb), (l2ob, l2ob_sb)):
            g.dma_start(out=t[:, :], in_=ap).then_inc(qg, 16)
        for k in range(8):
            g.dma_start(out=cpT_sb[:, k * 1024:(k + 1) * 1024],
                        in_=cpT[k * 128:(k + 1) * 128, :]).then_inc(qg, 16)

    @block.vector
    def _(v):
        v.wait_ge(qs, 32)
        v.reduce_sum(dvs[:, :], dv_sb[:, :], axis=mybir.AxisListType.X)
        v.tensor_copy(dvm_sb[:, :], dvs[:, :]).then_inc(v0, 1)

    @block.scalar
    def _(s):
        s.memzero(dmy0[:, :])
        s.wait_ge(t0, 1)
        s.activation(dmy0[:, :], dmy0[:, :], AF.Identity)
        s.activation(dmy0[:, :], dmy0[:, :], AF.Identity)
        s.activation(sc_b[:, :], ps_1[:, :], AF.Copy, scale=1.0 / DI, bias=1.0)
        s.activation(dmy0[:, :], sc_b[:, :], AF.Identity).then_inc(a0, 1)

    @block.tensor
    def _(t):
        t.wait_ge(qg, 16 * 12)
        t.wait_ge(v0, 1)
        t.matmul(ps_a[0:32, :], l2o_sb[0:64, :], cpT_sb[0:64, 0:512], start=True, stop=True)
        t.matmul(ps_1[:, :], dvs[:, :], onc_sb[:, :], start=True, stop=True).then_inc(t0, 1)
        t.wait_ge(a0, 1)
        t.matmul(ps_a[0:32, :], l2o_sb[0:64, :], cpT_sb[0:64, 0:512], start=True, stop=True)
        t.matmul(ps_c[:, 0:1], onr_sb[:, :], sc_b[:, :], start=True, stop=True).then_inc(t0, 1)
        for h in range(NH):
            base = 64 * (h % 2)
            for half in range(2):
                i = 2 * h + half
                ps = ps_a if i % 2 == 0 else ps_b
                if i >= 2:
                    t.wait_ge(v1, i - 1)
                t.matmul(ps[0:32, :], l2o_sb[base:base + 64, :],
                         cpT_sb[base:base + 64, (h // 2) * 1024 + half * 512:(h // 2) * 1024 + half * 512 + 512],
                         start=True, stop=True).then_inc(t1, 1)
        t.wait_ge(v2, 1)
        for m in range(8):
            for k in range(8):
                ins = t.matmul(ps_c[:, m:m + 1],
                               cpT_sb[:, k * 1024 + 128 * m:k * 1024 + 128 * m + 128],
                               l2ob_sb[:, k:k + 1], start=(k == 0), stop=(k == 7))
        ins.then_inc(t2, 1)

    @block.vector
    def _(v):
        v.wait_ge(t0, 2)
        v.tensor_copy(dvm_sb[:, :], ps_c[:, 0:1]).then_inc(v2, 1)
        for h in range(NH):
            gI = h // 4; b = h % 4
            for half in range(2):
                i = 2 * h + half
                v.wait_ge(t1, i + 1)
                ps = ps_a if i % 2 == 0 else ps_b
                v.tensor_copy(cph_sb[32 * b:32 * b + 32, gI * 1024 + half * 512:gI * 1024 + half * 512 + 512],
                              ps[0:32, :]).then_inc(v1, 1)
        v.wait_ge(t2, 1)
        v.tensor_add(cbias_sb[:, :], cpb_sb[:, :], ps_c[:, :]).then_inc(v2, 1)

    @block.sync
    def _(sy):
        sy.wait_ge(v2, 2)
        sy.wait_ge(v1, 32)
        sy.dma_start(out=dvm1, in_=dvm_sb[:, :]).then_inc(qs, 16)
        sy.dma_start(out=cbias, in_=cbias_sb[:, :]).then_inc(qs, 16)
        for gI in range(4):
            sy.dma_start(out=cphp[gI * 128:(gI + 1) * 128, :],
                         in_=cph_sb[:, gI * 1024:(gI + 1) * 1024]).then_inc(qs, 16)
        sy.wait_ge(qs, 16 * 8)



def emit_p1(ctx, block):
    T, TC, NT = ctx.T, ctx.TC, ctx.NT
    xf = ctx.inp("xf"); wiT = ctx.inp("wiT"); bi = ctx.inp("bi_c")
    xinb = ctx.park("xinb"); x32b = ctx.park("x32b"); szb = ctx.park("szb")

    xb = ctx.sbuf("p1_xb", [128, NM * T], BF16)
    wi_sb = ctx.sbuf("p1_wi", [128, NM * 4096], BF16)
    bi_sb = ctx.sbuf("p1_bi", [128, 32], F32)
    stg = [ctx.sbuf(f"p1_st{i}", [128, TC], BF16) for i in range(4)]
    st32 = [ctx.sbuf(f"p1_s32{i}", [32, TC], BF16) for i in range(4)]
    pss = [ctx.psum(f"p1_ps{i}", [128, TC], F32) for i in range(2)]

    qs = ctx.sem("p1_qs"); qg = ctx.sem("p1_qg"); qa = ctx.sem("p1_qa")
    mm = ctx.sem("p1_mm"); ev = ctx.sem("p1_ev"); e32 = ctx.sem("p1_e32")

    NG = 32 * NT
    STORES_BEFORE = []
    cnt = 0
    for m in range(32):
        for tc in range(NT):
            cnt += 1 + (1 if m == 0 else 0)
            STORES_BEFORE.append(cnt)

    @block.sync
    def _(sy):
        sy.dma_start(out=bi_sb[:, :], in_=bi).then_inc(qs, 16)

    @block.gpsimd
    def _(g):
        for k in range(NM):
            g.dma_start(out=xb[:, k * T:(k + 1) * T],
                        in_=xf[k * 128:(k + 1) * 128, :]).then_inc(qg, 16)
        for k in range(NM):
            g.dma_start(out=wi_sb[:, k * 4096:(k + 1) * 4096],
                        in_=wiT[k * 128:(k + 1) * 128, :]).then_inc(qg, 16)

    @block.tensor
    def _(t):
        t.wait_ge(qg, 16 * 2 * NM)
        gi = 0
        for m in range(32):
            for tc in range(NT):
                ps = pss[gi % 2]
                if gi >= 2:
                    t.wait_ge(ev, gi - 1)
                for k in range(NM):
                    ins = t.matmul(ps[:, :], wi_sb[:, k * 4096 + 128 * m:k * 4096 + 128 * m + 128],
                                   xb[:, k * T + tc * TC:k * T + tc * TC + TC],
                                   start=(k == 0), stop=(k == NM - 1))
                ins.then_inc(mm, 1)
                gi += 1

    @block.scalar
    def _(s):
        s.wait_ge(qs, 16)
        gi = 0
        for m in range(32):
            for tc in range(NT):
                ps = pss[gi % 2]
                s.wait_ge(mm, gi + 1)
                if gi >= 4:
                    s.wait_ge(qs, 16 * (1 + STORES_BEFORE[gi - 4]))
                buf = stg[gi % 4]
                fn = AF.Identity if m < 16 else AF.Silu
                ins = s.activation(buf[:, :], ps[:, :], fn, bias=bi_sb[:, m:m + 1])
                if m == 0:
                    ins = s.activation(st32[tc % 4][:, :], ps[0:32, :], AF.Identity, bias=bi_sb[0:32, 0:1])
                ins.then_inc(ev, 1)
                gi += 1

    @block.sync
    def _(sy):
        gi = 0
        for m in range(32):
            for tc in range(NT):
                sy.wait_ge(ev, gi + 1)
                buf = stg[gi % 4]
                if m < 16:
                    dst = xinb[m * 128:(m + 1) * 128, tc * TC:tc * TC + TC]
                else:
                    dst = szb[(m - 16) * 128:(m - 16 + 1) * 128, tc * TC:tc * TC + TC]
                sy.dma_start(out=dst, in_=buf[:, :]).then_inc(qs, 16)
                if m == 0:
                    sy.dma_start(out=x32b[:, tc * TC:tc * TC + TC],
                                 in_=st32[tc % 4][:, :]).then_inc(qs, 16)
                gi += 1
        sy.wait_ge(qs, 16 * (1 + NG + NT))



def emit_p2(ctx, block):
    T, TC, NT = ctx.T, ctx.TC, ctx.NT
    wdtT = ctx.inp("wdtT"); bdt = ctx.inp("bdt_c")
    cw = ctx.inp("convw_c"); cb = ctx.inp("convb_c")
    alog = ctx.inp("alog_c"); bm = ctx.inp("bm_c"); cm = ctx.inp("cm_c")
    xinb = ctx.park("xinb"); x32b = ctx.park("x32b"); szb = ctx.park("szb")
    yzb = ctx.park("yzb")

    wdt_sb = ctx.sbuf("p2_wdt", [32, DI], BF16)
    x32sb = ctx.sbuf("p2_x32", [32, T], BF16)
    bdt_sb = ctx.sbuf("p2_bdt", [128, 16], F32)
    cw_sb = ctx.sbuf("p2_cw", [128, 64], F32)
    cb_sb = ctx.sbuf("p2_cb", [128, 16], F32)
    alog_sb = ctx.sbuf("p2_alog", [128, 256], F32)
    bm_sb = ctx.sbuf("p2_bm", [128, 256], F32)
    cm_sb = ctx.sbuf("p2_cm", [128, 256], F32)
    acol = ctx.sbuf("p2_acol", [128, 256], F32)
    cbm = ctx.sbuf("p2_cbm", [128, 256], F32)
    xpad = [ctx.sbuf(f"p2_xp{i}", [128, T + 4], BF16) for i in range(2)]
    szt = [ctx.sbuf(f"p2_sz{i}", [128, T], BF16) for i in range(2)]
    yzo = [ctx.sbuf(f"p2_yz{i}", [128, T], BF16) for i in range(2)]
    upre = ctx.sbuf("p2_upre", [128, T], F32)
    ut = ctx.sbuf("p2_ut", [128, T], F32)
    dtt = ctx.sbuf("p2_dtt", [128, T], F32)
    dtu = ctx.sbuf("p2_dtu", [128, T], BF16)
    an = [ctx.sbuf(f"p2_an{i}", [128, T], BF16) for i in range(2)]
    bn = [ctx.sbuf(f"p2_bn{i}", [128, T], BF16) for i in range(2)]
    gn = [ctx.sbuf(f"p2_gn{i}", [128, T], BF16) for i in range(2)]
    yac = ctx.sbuf("p2_yac", [128, T], BF16)
    tmp = [ctx.sbuf(f"p2_tmp{i}", [128, TC], F32) for i in range(4)]
    pdt = [ctx.psum(f"p2_pdt{i}", [128, TC], F32) for i in range(2)]

    qs = ctx.sem("p2_qs"); qg = ctx.sem("p2_qg")
    mm = ctx.sem("p2_mm"); sp = ctx.sem("p2_sp")
    ab = ctx.sem("p2_ab")
    ad = ctx.sem("p2_ad")
    cv = ctx.sem("p2_cv")
    si = ctx.sem("p2_si")
    anr = ctx.sem("p2_an")
    sc = ctx.sem("p2_sc")
    fin = ctx.sem("p2_fin")
    pls = ctx.sem("p2_pls")
    pr = ctx.sem("p2_pr")

    @block.gpsimd
    def _(g):
        g.dma_start(out=wdt_sb[:, :], in_=wdtT).then_inc(qg, 16)
        g.dma_start(out=x32sb[:, :], in_=x32b).then_inc(qg, 16)


    @block.sync
    def _(sy):
        for ap, t in ((bdt, bdt_sb), (cw, cw_sb), (cb, cb_sb), (alog, alog_sb),
                      (bm, bm_sb), (cm, cm_sb)):
            sy.dma_start(out=t[:, :], in_=ap).then_inc(qs, 16)
        for db in range(NDB):
            if db >= 2:
                sy.wait_ge(cv, db - 1)
                sy.wait_ge(fin, db - 1)
            sy.dma_start(out=xpad[db % 2][:, 4:4 + T],
                         in_=xinb[db * 128:(db + 1) * 128, :]).then_inc(qs, 16)
            sy.dma_start(out=szt[db % 2][:, :],
                         in_=szb[db * 128:(db + 1) * 128, :]).then_inc(qs, 16)
            if db >= 1:
                sy.wait_ge(fin, db)
                sy.dma_start(out=yzb[(db - 1) * 128:db * 128, :],
                             in_=yzo[(db - 1) % 2][:, :]).then_inc(qs, 16)
        sy.wait_ge(fin, NDB)
        sy.dma_start(out=yzb[(NDB - 1) * 128:NDB * 128, :],
                     in_=yzo[(NDB - 1) % 2][:, :]).then_inc(qs, 16)
        sy.wait_ge(qs, 16 * (6 + 3 * NDB))

    @block.scalar
    def _(s):
        s.wait_ge(qs, 16 * 6)
        s.activation(acol[:, :], alog_sb[:, :], AF.Exp).then_inc(pr, 1)
        for db in range(NDB):
            for tcq in range(NT):
                i = db * NT + tcq
                s.wait_ge(mm, i + 1)
                if i >= 2:
                    s.wait_ge(ad, i - 1)
                t1 = tmp[(i % 2) * 2]; t2 = tmp[(i % 2) * 2 + 1]
                ps = pdt[i % 2]
                s.activation(t1[:, :], ps[:, :], AF.Abs, bias=bdt_sb[:, db:db + 1])
                s.activation(t1[:, :], t1[:, :], AF.Exp, scale=-1.0)
                s.activation(t1[:, :], t1[:, :], AF.Ln, bias=1.0)
                ins = s.activation(t2[:, :], ps[:, :], AF.Relu, bias=bdt_sb[:, db:db + 1])
                ins.then_inc(ab, 1)
            s.wait_ge(cv, db + 1)
            s.activation(ut[:, :], upre[:, :], AF.Silu, bias=cb_sb[:, db:db + 1]).then_inc(si, 1)
            s.wait_ge(ctx.sem("p2_prv"), 1)
            s.wait_ge(ad, (db + 1) * NT)
            for n in range(DS):
                ni = db * DS + n
                if ni >= 2:
                    s.wait_ge(sc, ni - 1)
                s.activation(an[ni % 2][:, :], dtt[:, :], AF.Exp,
                             scale=acol[:, db * 16 + n:db * 16 + n + 1]).then_inc(anr, 1)

    @block.tensor
    def _(t):
        t.wait_ge(qg, 32)
        for db in range(NDB):
            for tcq in range(NT):
                i = db * NT + tcq
                if i >= 2:
                    t.wait_ge(ab, i - 1)
                t.matmul(pdt[i % 2][:, :], wdt_sb[:, db * 128:(db + 1) * 128],
                         x32sb[:, tcq * TC:tcq * TC + TC],
                         start=True, stop=True).then_inc(mm, 1)

    @block.vector
    def _(v):
        v.wait_ge(pr, 1)
        v.tensor_scalar(acol[:, :], acol[:, :], -1.0, None, ALU.mult)
        v.wait_ge(ctx.sem("p2_qs"), 16 * 6)
        v.tensor_mul(cbm[:, :], cm_sb[:, :], bm_sb[:, :]).then_inc(ctx.sem("p2_prv"), 1)
        for db in range(NDB):
            v.wait_ge(qs, 16 * (6 + 2 * (db + 1)))
            if db >= 1:
                v.wait_ge(si, db)
            v.memset(xpad[db % 2][:, 0:4], 0.0)
            v.tensor_scalar(upre[:, :], xpad[db % 2][:, 4:4 + T],
                            cw_sb[:, db * 4 + 3:db * 4 + 4], None, ALU.mult)
            for o in (1, 2, 3):
                ins = v.scalar_tensor_tensor(upre[:, :], xpad[db % 2][:, 4 - o:4 - o + T],
                                             cw_sb[:, db * 4 + 3 - o:db * 4 + 4 - o],
                                             upre[:, :], ALU.mult, ALU.add)
            ins.then_inc(cv, 1)
            for tcq in range(NT):
                i = db * NT + tcq
                v.wait_ge(ab, i + 1)
                v.tensor_add(dtt[:, tcq * TC:tcq * TC + TC],
                             tmp[(i % 2) * 2][:, :], tmp[(i % 2) * 2 + 1][:, :]).then_inc(ad, 1)
            v.wait_ge(si, db + 1)
            v.tensor_mul(dtu[:, :], dtt[:, :], ut[:, :])
            for n in range(DS):
                ni = db * DS + n
                v.tensor_scalar(bn[ni % 2][:, :], dtu[:, :],
                                cbm[:, db * 16 + n:db * 16 + n + 1], None, ALU.mult)
                v.wait_ge(anr, ni + 1)
                v.tensor_tensor_scan(gn[ni % 2][:, :], an[ni % 2][:, :], bn[ni % 2][:, :],
                                     0.0, ALU.mult, ALU.add).then_inc(sc, 1)
                if n == 0:
                    v.tensor_copy(yac[:, :], gn[ni % 2][:, :])
                else:
                    v.tensor_add(yac[:, :], yac[:, :], gn[ni % 2][:, :])
            if db >= 2:
                v.wait_ge(qs, 16 * (3 * db + 5))
            v.tensor_mul(yzo[db % 2][:, :], yac[:, :], szt[db % 2][:, :]).then_inc(fin, 1)



def emit_p3(ctx, block):
    T, TC, NT = ctx.T, ctx.TC, ctx.NT
    xf = ctx.inp("xf"); woT = ctx.inp("woT"); bo = ctx.inp("bo_c")
    onc = ctx.inp("ones_col"); onr = ctx.inp("ones_row")
    yzb = ctx.park("yzb"); dvm1 = ctx.park("dvm1")
    x1f = ctx.park("x1f"); x1b = ctx.park("x1b")

    wo_sb = ctx.sbuf("p3_wo", [128, NDB * 1024], BF16)
    xff = [ctx.sbuf(f"p3_xf{i}", [128, NM * TC], F32) for i in range(2)]
    yzt = [ctx.sbuf(f"p3_yz{i}", [128, NDB * TC], BF16) for i in range(2)]
    bo_sb = ctx.sbuf("p3_bo", [128, 8], F32)
    dvm_sb = ctx.sbuf("p3_dvm", [128, 1], F32)
    onc_sb = ctx.sbuf("p3_onc", [128, 1], BF16)
    onr_sb = ctx.sbuf("p3_onr", [1, 512], BF16)
    tmpE = [ctx.sbuf(f"p3_te{i}", [128, TC], F32) for i in range(2)]
    res = ctx.sbuf("p3_res", [128, NM * TC], F32)
    resb = ctx.sbuf("p3_resb", [128, NM * TC], BF16)
    sqb = ctx.sbuf("p3_sqb", [128, NM * TC], BF16)
    xstg = [ctx.sbuf(f"p3_xs{i}", [128, TC], F32) for i in range(4)]
    mrow = ctx.sbuf("p3_mrow", [1, TC], F32)
    vrow = ctx.sbuf("p3_vrow", [1, TC], F32)
    srow = ctx.sbuf("p3_srow", [1, TC], F32)
    rrow = ctx.sbuf("p3_rrow", [1, TC], F32)
    msq = ctx.sbuf("p3_msq", [1, TC], F32)
    mrow_b = ctx.sbuf("p3_mrb", [1, TC], BF16)
    rrow_b = ctx.sbuf("p3_rrb", [1, TC], BF16)
    psA = [ctx.psum(f"p3_pa{i}", [128, TC], F32) for i in range(2)]
    psS = ctx.psum("p3_ps", [1, TC], F32)
    psQ = ctx.psum("p3_pq", [1, TC], F32)
    psM = ctx.psum("p3_pm", [128, TC], F32)
    psR = ctx.psum("p3_pr", [128, TC], F32)

    qs = ctx.sem("p3_qs"); qg = ctx.sem("p3_qg")
    mm = ctx.sem("p3_mm"); st = ctx.sem("p3_st"); bc = ctx.sem("p3_bc")
    ae = ctx.sem("p3_ae"); sq = ctx.sem("p3_sq")
    ar1 = ctx.sem("p3_ar1"); ar2 = ctx.sem("p3_ar2")
    rv = ctx.sem("p3_rv"); rb = ctx.sem("p3_rb"); dv2 = ctx.sem("p3_dv2")
    rw = ctx.sem("p3_rw"); xn = ctx.sem("p3_xn")

    @block.sync
    def _(sy):
        for ap, tile in ((dvm1, dvm_sb), (bo, bo_sb)):
            sy.dma_start(out=tile[:, :], in_=ap).then_inc(qs, 16)
        sy.dma_start(out=onc_sb[:, :], in_=onc).then_inc(qs, 16)
        sy.dma_start(out=onr_sb[:, :], in_=onr).then_inc(qs, 16)
        for tc in range(NT):
            if tc >= 2:
                sy.wait_ge(rv, 8 * (tc - 1))
                sy.wait_ge(mm, 8 * (tc - 1))
            for m in range(NM):
                sy.dma_start(out=xff[tc % 2][:, m * TC:(m + 1) * TC],
                             in_=xf[m * 128:(m + 1) * 128, tc * TC:tc * TC + TC]).then_inc(qs, 16)
            for k in range(NDB):
                sy.dma_start(out=yzt[tc % 2][:, k * TC:(k + 1) * TC],
                             in_=yzb[k * 128:(k + 1) * 128, tc * TC:tc * TC + TC]).then_inc(qs, 16)

    @block.gpsimd
    def _(g):
        for k in range(NDB):
            g.dma_start(out=wo_sb[:, k * 1024:(k + 1) * 1024],
                        in_=woT[k * 128:(k + 1) * 128, :]).then_inc(qg, 16)
        for tc in range(NT):
            for m in range(NM):
                idx = tc * NM + m
                g.wait_ge(xn, idx + 1)
                b = xstg[idx % 4]
                g.dma_start(out=x1b[m * 128:(m + 1) * 128, tc * TC:tc * TC + TC],
                            in_=b[:, :]).then_inc(qg, 16)
                g.dma_start(out=x1f[m * 128:(m + 1) * 128, tc * TC:tc * TC + TC],
                            in_=b[:, :]).then_inc(qg, 16)
        g.wait_ge(qg, 16 * (NDB + 2 * NM * NT))

    @block.tensor
    def _(t):
        t.wait_ge(qg, 16 * NDB)
        for tc in range(NT):
            t.wait_ge(qs, 16 * (4 + (NM + NDB) * (tc + 1)))
            for m in range(NM):
                idx = tc * NM + m
                if idx >= 2:
                    t.wait_ge(ae, idx - 1)
                for k in range(NDB):
                    ins = t.matmul(psA[idx % 2][:, :],
                                   wo_sb[:, k * 1024 + 128 * m:k * 1024 + 128 * m + 128],
                                   yzt[tc % 2][:, k * TC:(k + 1) * TC],
                                   start=(k == 0), stop=(k == NDB - 1))
                ins.then_inc(mm, 1)
            t.wait_ge(rb, tc + 1)
            t.wait_ge(sq, tc + 1)
            for m in range(NM):
                t.matmul(psS[:, :], onc_sb[:, :], resb[:, m * TC:(m + 1) * TC],
                         start=(m == 0), stop=(m == NM - 1))
            for m in range(NM):
                ins = t.matmul(psQ[:, :], onc_sb[:, :], sqb[:, m * TC:(m + 1) * TC],
                               start=(m == 0), stop=(m == NM - 1))
            ins.then_inc(st, 1)
            t.wait_ge(rw, tc + 1)
            t.matmul(psM[:, :], onr_sb[0:1, 0:128], mrow_b[:, :], start=True, stop=True)
            t.matmul(psR[:, :], onr_sb[0:1, 0:128], rrow_b[:, :], start=True, stop=True).then_inc(bc, 1)

    @block.scalar
    def _(s):
        s.wait_ge(qs, 64)
        for tc in range(NT):
            for m in range(NM):
                idx = tc * NM + m
                s.wait_ge(mm, idx + 1)
                if idx >= 2:
                    s.wait_ge(rv, idx - 1)
                s.activation(tmpE[idx % 2][:, :], psA[idx % 2][:, :], AF.Identity,
                             bias=bo_sb[:, m:m + 1]).then_inc(ae, 1)
            s.wait_ge(rb, tc + 1)
            s.activation(sqb[:, :], resb[:, :], AF.Square).then_inc(sq, 1)
            s.wait_ge(st, tc + 1)
            s.activation(mrow[:, :], psS[:, :], AF.Identity, scale=1.0 / DM)
            s.activation(vrow[:, :], psQ[:, :], AF.Identity, scale=1.0 / DM).then_inc(ar1, 1)
            s.wait_ge(dv2, tc + 1)
            s.activation(srow[:, :], vrow[:, :], AF.Sqrt).then_inc(ar2, 1)

    @block.vector
    def _(v):
        v.wait_ge(qs, 32)
        for tc in range(NT):
            for m in range(NM):
                idx = tc * NM + m
                v.wait_ge(ae, idx + 1)
                v.scalar_tensor_tensor(res[:, m * TC:(m + 1) * TC],
                                       xff[tc % 2][:, m * TC:(m + 1) * TC],
                                       dvm_sb[:, 0:1], tmpE[idx % 2][:, :],
                                       ALU.mult, ALU.add).then_inc(rv, 1)
                ins = v.tensor_copy(resb[:, m * TC:(m + 1) * TC], res[:, m * TC:(m + 1) * TC])
            ins.then_inc(rb, 1)
            v.wait_ge(ar1, tc + 1)
            v.tensor_mul(msq[:, :], mrow[:, :], mrow[:, :])
            v.scalar_tensor_tensor(vrow[:, :], vrow[:, :], 1e-5, msq[:, :],
                                   ALU.add, ALU.subtract).then_inc(dv2, 1)
            v.wait_ge(ar2, tc + 1)
            v.reciprocal(rrow[:, :], srow[:, :])
            v.tensor_copy(mrow_b[:, :], mrow[:, :])
            v.tensor_copy(rrow_b[:, :], rrow[:, :]).then_inc(rw, 1)
            v.wait_ge(bc, tc + 1)
            for m in range(NM):
                idx = tc * NM + m
                if idx >= 4:
                    v.wait_ge(qg, 16 * (NDB + 2 * (idx - 3)))
                v.tensor_sub(xstg[idx % 4][:, :], res[:, m * TC:(m + 1) * TC], psM[:, :])
                v.tensor_mul(xstg[idx % 4][:, :], xstg[idx % 4][:, :], psR[:, :]).then_inc(xn, 1)



def emit_ln_sum(ctx, block, pref, parkA, parkB, dstF, dstB, pre_qg=0, pre_qs=0):
    T, TC, NT = ctx.T, ctx.TC, ctx.NT
    onc = ctx.inp("ones_col"); onr = ctx.inp("ones_row")
    nc = ctx.nc
    av = [ctx.sbuf(f"{pref}_a{i}", [128, NM * TC], F32) for i in range(2)]
    bv = [ctx.sbuf(f"{pref}_b{i}", [128, NM * TC], F32) for i in range(2)]
    res = ctx.sbuf(f"{pref}_res", [128, NM * TC], F32)
    resb = ctx.sbuf(f"{pref}_resb", [128, NM * TC], BF16)
    sqb = ctx.sbuf(f"{pref}_sqb", [128, NM * TC], BF16)
    xstg = [ctx.sbuf(f"{pref}_xs{i}", [128, TC], F32) for i in range(4)]
    onc_sb = ctx.sbuf(f"{pref}_onc", [128, 1], BF16)
    onr_sb = ctx.sbuf(f"{pref}_onr", [1, 512], BF16)
    mrow = ctx.sbuf(f"{pref}_mrow", [1, TC], F32)
    vrow = ctx.sbuf(f"{pref}_vrow", [1, TC], F32)
    srow = ctx.sbuf(f"{pref}_srow", [1, TC], F32)
    rrow = ctx.sbuf(f"{pref}_rrow", [1, TC], F32)
    msq = ctx.sbuf(f"{pref}_msq", [1, TC], F32)
    mrow_b = ctx.sbuf(f"{pref}_mrb", [1, TC], BF16)
    rrow_b = ctx.sbuf(f"{pref}_rrb", [1, TC], BF16)
    psS = [ctx.psum(f"{pref}_ps{i}", [1, TC], F32) for i in range(2)]
    psQ = [ctx.psum(f"{pref}_pq{i}", [1, TC], F32) for i in range(2)]
    psM = [ctx.psum(f"{pref}_pm{i}", [128, TC], F32) for i in range(2)]
    psR = [ctx.psum(f"{pref}_pr{i}", [128, TC], F32) for i in range(2)]

    qs = ctx.sem(f"{pref}_qs2"); qg = ctx.sem(f"{pref}_qg2")
    st = ctx.sem(f"{pref}_st"); bc = ctx.sem(f"{pref}_bc")
    sq = ctx.sem(f"{pref}_sq"); ar1 = ctx.sem(f"{pref}_ar1"); ar2 = ctx.sem(f"{pref}_ar2")
    rb = ctx.sem(f"{pref}_rb"); dv2 = ctx.sem(f"{pref}_dv2")
    rw = ctx.sem(f"{pref}_rw"); xn = ctx.sem(f"{pref}_xn")
    nst = 2 if dstB is not None else 1

    @block.sync
    def _(sy):
        sy.dma_start(out=onc_sb[:, :], in_=onc).then_inc(qs, 16)
        sy.dma_start(out=onr_sb[:, :], in_=onr).then_inc(qs, 16)
        for tc in range(NT):
            if tc >= 2:
                sy.wait_ge(rb, tc - 1)
            for m in range(NM):
                sy.dma_start(out=av[tc % 2][:, m * TC:(m + 1) * TC],
                             in_=parkA[m * 128:(m + 1) * 128, tc * TC:tc * TC + TC]).then_inc(qs, 16)
                sy.dma_start(out=bv[tc % 2][:, m * TC:(m + 1) * TC],
                             in_=parkB[m * 128:(m + 1) * 128, tc * TC:tc * TC + TC]).then_inc(qs, 16)

    @block.tensor
    def _(t):
        for tc in range(NT):
            t.wait_ge(rb, tc + 1)
            t.wait_ge(sq, tc + 1)
            if tc >= 2:
                t.wait_ge(ar1, tc - 1)
            for m in range(NM):
                t.matmul(psS[tc % 2][:, :], onc_sb[:, :], resb[:, m * TC:(m + 1) * TC],
                         start=(m == 0), stop=(m == NM - 1))
            for m in range(NM):
                ins = t.matmul(psQ[tc % 2][:, :], onc_sb[:, :], sqb[:, m * TC:(m + 1) * TC],
                               start=(m == 0), stop=(m == NM - 1))
            ins.then_inc(st, 1)
            t.wait_ge(rw, tc + 1)
            if tc >= 2:
                t.wait_ge(xn, NM * (tc - 1))
            t.matmul(psM[tc % 2][:, :], onr_sb[0:1, 0:128], mrow_b[:, :], start=True, stop=True)
            t.matmul(psR[tc % 2][:, :], onr_sb[0:1, 0:128], rrow_b[:, :], start=True, stop=True).then_inc(bc, 1)

    @block.scalar
    def _(s):
        for tc in range(NT):
            s.wait_ge(rb, tc + 1)
            s.activation(sqb[:, :], resb[:, :], AF.Square).then_inc(sq, 1)
            s.wait_ge(st, tc + 1)
            s.activation(mrow[:, :], psS[tc % 2][:, :], AF.Identity, scale=1.0 / DM)
            s.activation(vrow[:, :], psQ[tc % 2][:, :], AF.Identity, scale=1.0 / DM).then_inc(ar1, 1)
            s.wait_ge(dv2, tc + 1)
            s.activation(srow[:, :], vrow[:, :], AF.Sqrt).then_inc(ar2, 1)

    @block.vector
    def _(v):
        for tc in range(NT):
            v.wait_ge(qs, pre_qs + 16 * (2 + 2 * NM * (tc + 1)))
            v.tensor_add(res[:, :], av[tc % 2][:, :], bv[tc % 2][:, :])
            v.tensor_copy(resb[:, :], res[:, :]).then_inc(rb, 1)
            v.wait_ge(ar1, tc + 1)
            v.tensor_mul(msq[:, :], mrow[:, :], mrow[:, :])
            v.scalar_tensor_tensor(vrow[:, :], vrow[:, :], 1e-5, msq[:, :],
                                   ALU.add, ALU.subtract).then_inc(dv2, 1)
            v.wait_ge(ar2, tc + 1)
            v.reciprocal(rrow[:, :], srow[:, :])
            v.tensor_copy(mrow_b[:, :], mrow[:, :])
            v.tensor_copy(rrow_b[:, :], rrow[:, :]).then_inc(rw, 1)
            v.wait_ge(bc, tc + 1)
            for m in range(NM):
                idx = tc * NM + m
                if idx >= 4:
                    v.wait_ge(qg, pre_qg + 16 * nst * (idx - 3))
                v.tensor_sub(xstg[idx % 4][:, :], res[:, m * TC:(m + 1) * TC], psM[tc % 2][:, :])
                v.tensor_mul(xstg[idx % 4][:, :], xstg[idx % 4][:, :], psR[tc % 2][:, :]).then_inc(xn, 1)

    @block.gpsimd
    def _(g):
        for tc in range(NT):
            for m in range(NM):
                idx = tc * NM + m
                g.wait_ge(xn, idx + 1)
                b = xstg[idx % 4]
                g.dma_start(out=dstF[m * 128:(m + 1) * 128, tc * TC:tc * TC + TC],
                            in_=b[:, :]).then_inc(qg, 16)
                if dstB is not None:
                    g.dma_start(out=dstB[m * 128:(m + 1) * 128, tc * TC:tc * TC + TC],
                                in_=b[:, :]).then_inc(qg, 16)
        g.wait_ge(qg, pre_qg + 16 * nst * NM * NT)



def emit_p4(ctx, block):
    T, TC, NT = ctx.T, ctx.TC, ctx.NT
    NTT = T // 128
    wg = ctx.inp("wg"); bg = ctx.inp("bg_c")
    w1 = ctx.inp("w1"); b1 = ctx.inp("b1_c"); w2 = ctx.inp("w2"); b2 = ctx.inp("b2_c")
    identf = ctx.inp("identf"); onr = ctx.inp("ones_row")
    x1f = ctx.park("x1f"); x1b = ctx.park("x1b")
    wrow = ctx.park("wrow"); facc = ctx.park("facc")
    x2f = ctx.park("x2f"); x2b = ctx.park("x2b")

    qs = ctx.sem("p4_qs"); qg = ctx.sem("p4_qg")

    _gate_scope = ctx.nc.named_scope("p4gate")
    _gate_scope.__enter__()
    gstk = ExitStack()
    sb = lambda n, sh, dt=F32: gstk.enter_context(ctx.nc.sbuf_tensor(n, sh, dt))
    psm = lambda n, sh: gstk.enter_context(ctx.nc.psum_tensor(n, sh, F32))

    wg_sb = ctx.sbuf("p4_wg", [128, NM * 8], F32)
    bg_sb = ctx.sbuf("p4_bg", [8, 1], F32)
    onr_sb = ctx.sbuf("p4_onr", [1, 512], BF16)
    x1t = [ctx.sbuf(f"p4_x1t{i}", [128, NM * TC], BF16) for i in range(2)]
    wbc = ctx.sbuf("p4_wbc", [128, T], BF16)

    x1g = [gstk.enter_context(ctx.nc.sbuf_tensor(f"p4_x1g{i}", [128, NM * TC], F32)) for i in range(2)]
    idf_sb = sb("p4_idf", [128, 128])
    lgs = sb("p4_lgs", [8, T])
    ltok = sb("p4_ltok", [128, 8 * NTT])
    wtok = sb("p4_wtok", [128, 8 * NTT])
    srt = sb("p4_srt", [128, 8 * NTT])
    shf1 = sb("p4_shf1", [128, 8 * NTT]); shf2 = sb("p4_shf2", [128, 8 * NTT])
    eea = sb("p4_eea", [128, 8 * NTT]); sela = sb("p4_sela", [128, 8 * NTT])
    dena = sb("p4_dena", [128, NTT]); rdena = sb("p4_rdena", [128, NTT])
    red4 = sb("p4_red4", [128, 4 * NTT]); red2 = sb("p4_red2", [128, 2 * NTT])
    dmy = sb("p4_dmy", [128, 1])
    NBW = (8 * NTT + 127) // 128
    wtr = sb("p4_wtr", [128, NBW * 128])
    psG = psm("p4_psG", [8, TC])
    psTT = psm("p4_psTT", [128, 8 * NTT])
    psT2T = psm("p4_psT2T", [128, NBW * 128])
    psD = psm("p4_psD", [128, 128])

    gmm = ctx.sem("p4_gmm"); gae = ctx.sem("p4_gae")
    gtp = ctx.sem("p4_gtp"); gdv = ctx.sem("p4_gdv")
    gex = ctx.sem("p4_gex"); gxd = ctx.sem("p4_gxd")
    gwv = ctx.sem("p4_gwv"); gt2 = ctx.sem("p4_gt2"); gw2 = ctx.sem("p4_gw2")

    @block.sync
    def _(sy):
        sy.dma_start(out=bg_sb[:, :], in_=bg).then_inc(qs, 16)
        sy.dma_start(out=idf_sb[:, :], in_=identf).then_inc(qs, 16)
        for tc in range(NT):
            if tc >= 2:
                sy.wait_ge(gmm, tc - 1)
            for m in range(NM):
                sy.dma_start(out=x1g[tc % 2][:, m * TC:(m + 1) * TC],
                             in_=x1f[m * 128:(m + 1) * 128, tc * TC:tc * TC + TC]).then_inc(qs, 16)
        sy.wait_ge(gw2, 1)
        for i in range(NTT):
            c = (8 * i) // 128
            r = 8 * i - 128 * c
            sy.dma_start(out=wrow[:, i * 128:(i + 1) * 128],
                         in_=wtr[r:r + 8, c * 128:(c + 1) * 128]).then_inc(qs, 16)

    @block.gpsimd
    def _(g):
        for m in range(NM):
            g.dma_start(out=wg_sb[:, m * 8:(m + 1) * 8],
                        in_=wg[m * 128:(m + 1) * 128, :]).then_inc(qg, 16)
        g.dma_start(out=onr_sb[:, :], in_=onr).then_inc(qg, 16)

    @block.tensor
    def _(t):
        t.wait_ge(qg, 16 * (NM + 1))
        for tc in range(NT):
            t.wait_ge(qs, 16 * (2 + NM * (tc + 1)))
            if tc >= 1:
                t.wait_ge(gae, tc)
            for m in range(NM):
                ins = t.matmul(psG[:, :], wg_sb[:, m * 8:(m + 1) * 8],
                               x1g[tc % 2][:, m * TC:(m + 1) * TC],
                               start=(m == 0), stop=(m == NM - 1))
            ins.then_inc(gmm, 1)
        for i in range(NTT):
            t.wait_ge(gae, (i * 128) // TC + 1)
            t.transpose(psTT[:, 8 * i:8 * i + 8], lgs[:, i * 128:(i + 1) * 128],
                        idf_sb[0:8, 0:8]).then_inc(gtp, 1)
        t.transpose(psD[:, 0:8], lgs[:, 0:128], idf_sb[0:8, 0:8]).then_inc(gtp, 1)
        t.wait_ge(gwv, 1)
        for c in range(NBW):
            wc = min(128, 8 * NTT - c * 128)
            t.transpose(psT2T[0:wc, c * 128:(c + 1) * 128],
                        wtok[:, c * 128:c * 128 + wc],
                        idf_sb[:, :]).then_inc(gt2, 1)
        t.transpose(psD[0:8, :], wtok[:, 0:8], idf_sb[:, :]).then_inc(gt2, 1)

    @block.scalar
    def _(s):
        s.wait_ge(qs, 16)
        for tc in range(NT):
            s.wait_ge(gmm, tc + 1)
            s.activation(lgs[:, tc * TC:tc * TC + TC], psG[:, :], AF.Identity,
                         bias=bg_sb[:, 0:1]).then_inc(gae, 1)
        s.wait_ge(gxd, 1)
        s.activation(dmy[:, :], dmy[:, :], AF.Identity)
        s.activation(dmy[:, :], dmy[:, :], AF.Identity)
        s.activation(eea[:, :], shf1[:, :], AF.Exp)
        s.activation(sela[:, :], shf2[:, :], AF.Sign).then_inc(gex, 1)

    @block.vector
    def _(v):
        v.memzero(dmy[:, :])
        v.wait_ge(gtp, NTT + 1)
        v.tensor_copy(ltok[:, :], psTT[:, :])
        lt3 = ltok[:, :].rearrange("p (n k) -> p n k", k=8)
        srt3 = srt[:, :].rearrange("p (n k) -> p n k", k=8)
        for i in range(NTT):
            v.max(out=srt[:, 8 * i:8 * i + 8], in_=ltok[:, 8 * i:8 * i + 8])
        v.tensor_copy(dmy[:, :], dmy[:, :])
        v.tensor_copy(dmy[:, :], dmy[:, :])
        v.tensor_copy(dmy[:, :], dmy[:, :])
        v.tensor_sub(shf1[:, :].rearrange("p (n k) -> p n k", k=8), lt3,
                     srt3[:, :, 0:1].broadcast_to([128, NTT, 8]))
        v.tensor_sub(shf2[:, :].rearrange("p (n k) -> p n k", k=8), lt3,
                     srt3[:, :, 1:2].broadcast_to([128, NTT, 8]))
        v.tensor_copy(dmy[:, :], shf2[:, 0:1]).then_inc(gxd, 1)
        v.wait_ge(gex, 1)

        def vdly():
            v.tensor_copy(dmy[:, :], dmy[:, :])
            v.tensor_copy(dmy[:, :], dmy[:, :])
            v.tensor_copy(dmy[:, :], dmy[:, :])

        vdly()
        v.tensor_scalar(sela[:, :], sela[:, :], 1.0, 1.0, ALU.add, ALU.min)
        vdly()
        v.tensor_mul(sela[:, :], sela[:, :], eea[:, :])
        vdly()

        def tree_red(src_t, dst, op):
            s3 = src_t[:, :].rearrange("p (n k) -> p n k", k=8)
            r43 = red4[:, :].rearrange("p (n k) -> p n k", k=4)
            r23 = red2[:, :].rearrange("p (n k) -> p n k", k=2)
            v.tensor_tensor(r43, s3[:, :, 0:8:2], s3[:, :, 1:8:2], op)
            vdly()
            v.tensor_tensor(r23, r43[:, :, 0:4:2], r43[:, :, 1:4:2], op)
            vdly()
            v.tensor_tensor(dst[:, :].rearrange("p (n k) -> p n k", k=1),
                            r23[:, :, 0:2:2], r23[:, :, 1:2:2], op)
            vdly()

        tree_red(sela, dena, ALU.add)
        v.reciprocal(rdena[:, :], dena[:, :])
        vdly()
        v.tensor_mul(wtok[:, :].rearrange("p (n k) -> p n k", k=8),
                     sela[:, :].rearrange("p (n k) -> p n k", k=8),
                     rdena[:, :].unsqueeze(2).broadcast_to([128, NTT, 8]))
        vdly()
        v.tensor_copy(dmy[:, :], wtok[:, 0:1]).then_inc(gwv, 1)
        v.wait_ge(gt2, NBW + 1)
        for c in range(NBW):
            wc = min(128, 8 * NTT - c * 128)
            ins = v.tensor_copy(wtr[0:wc, c * 128:(c + 1) * 128],
                                psT2T[0:wc, c * 128:(c + 1) * 128])
        ins.then_inc(gw2, 1)

    if DBG_GATE:
        dbgt = {}
        for n, tile in (("ltok", ltok), ("srt", srt), ("eea", eea), ("sela", sela),
                        ("wtok", wtok), ("dena", dena), ("rdena", rdena), ("lgs", lgs)):
            dbgt[n] = (ctx.nc.dram_tensor("dbg_" + n, list(tile.shape), F32,
                                          kind="ExternalOutput").ap(), tile)

        @block.sync
        def _(sy):
            sy.wait_ge(ctx.sem("p4_gw2"), 1)
            sy.nop(cycle_cnt=2000)
            for n, (ap, tile) in dbgt.items():
                sy.dma_start(out=ap, in_=tile[:, :]).then_inc(qs, 16)
            sy.wait_ge(qs, 16 * (2 + NM * NT + NTT + len(dbgt)))
        QS0_EXTRA = len(dbgt)
    else:
        QS0_EXTRA = 0

    gstk.close()
    _gate_scope.__exit__(None, None, None)
    _exp_scope = ctx.nc.named_scope("p4exp")
    _exp_scope.__enter__()

    estk = ExitStack()
    sbe = lambda n, sh, dt: estk.enter_context(ctx.nc.sbuf_tensor(n, sh, dt))
    psme = lambda n, sh: estk.enter_context(ctx.nc.psum_tensor(n, sh, F32))

    w1_sb = sbe("p4_w1", [128, NM * FF], BF16)
    w2_sb = sbe("p4_w2", [128, 32 * DM], BF16)
    b1_sb = [sbe(f"p4_b1{i}", [128, 32], F32) for i in range(2)]
    b2_sb = [sbe(f"p4_b2{i}", [128, 8], F32) for i in range(2)]
    web = sbe("p4_web", [1, T], BF16)
    hsb = sbe("p4_hsb", [128, 32 * TC], BF16)
    yt = [sbe(f"p4_yt{i}", [128, TC], F32) for i in range(2)]
    ysc = [sbe(f"p4_ysc{i}", [128, TC], F32) for i in range(4)]
    pH = [psme(f"p4_pH{i}", [128, TC]) for i in range(2)]
    pY = [psme(f"p4_pY{i}", [128, TC]) for i in range(2)]
    pW = [psme(f"p4_pW{i}", [128, TC]) for i in range(2)]

    hmm = ctx.sem("p4_hmm"); hv = ctx.sem("p4_hv")
    ymm = ctx.sem("p4_ymm"); yae = ctx.sem("p4_yae"); ydv = ctx.sem("p4_ydv")
    wbm = ctx.sem("p4_wbm"); wbe = ctx.sem("p4_wbe")

    QS0 = 2 + NM * NT + NTT + QS0_EXTRA
    SE = NM * NT + 2
    QG0 = NM + 1
    GE = 41 + NM * NT

    QG_W1 = {}; QG_W2 = {}; QG_WEB = {}; QG_ACC = {}
    _c = QG0
    for e in range(NE_RUN):
        if e == 0:
            _c += NM; QG_W1[0] = _c
        _c += 32; QG_W2[e] = _c
        _c += 1; QG_WEB[e] = _c
        for tc in range(NT):
            for dm in range(NM):
                slot = (e * NT + tc) * NM + dm
                _c += 1; QG_ACC[slot] = _c
                if tc == NT - 1 and dm == 0 and e + 1 < NE_RUN:
                    _c += NM; QG_W1[e + 1] = _c
    QG_TOT = _c

    @block.gpsimd
    def _(g):
        for e in range(NE_RUN):
            if e == 0:
                g.wait_ge(ctx.sem("p4_gw2"), 1)
                g.wait_ge(ctx.sem("p4_qs"), 16 * QS0)
                for k in range(NM):
                    g.dma_start(out=w1_sb[:, k * FF:(k + 1) * FF],
                                in_=w1[e, k * 128:(k + 1) * 128, :]).then_inc(qg, 16)
            if e >= 1:
                g.wait_ge(ymm, 8 * NT * e)
            for k in range(32):
                g.dma_start(out=w2_sb[:, k * DM:(k + 1) * DM],
                            in_=w2[e, k * 128:(k + 1) * 128, :]).then_inc(qg, 16)
            if e >= 1:
                g.wait_ge(wbm, NT * e)
            g.dma_start(out=web[:, :], in_=wrow[e:e + 1, :]).then_inc(qg, 16)
            for tc in range(NT):
                for dm in range(NM):
                    slot = (e * NT + tc) * NM + dm
                    g.wait_ge(ydv, slot + 1)
                    op = {} if e == 0 else {"accum_op": ALU.add}
                    g.dma_start(out=facc[dm * 128:(dm + 1) * 128, tc * TC:tc * TC + TC],
                                in_=ysc[slot % 4][:, :], **op).then_inc(qg, 16)
                    if tc == NT - 1 and dm == 0 and e + 1 < NE_RUN:
                        g.wait_ge(hmm, 32 * NT * (e + 1))
                        for k in range(NM):
                            g.dma_start(out=w1_sb[:, k * FF:(k + 1) * FF],
                                        in_=w1[e + 1, k * 128:(k + 1) * 128, :]).then_inc(qg, 16)

    @block.sync
    def _(sy):
        for e in range(NE_RUN):
            if e >= 2:
                sy.wait_ge(hv, 32 * NT * (e - 1))
                sy.wait_ge(yae, NM * NT * (e - 1))
            sy.dma_start(out=b1_sb[e % 2][:, :], in_=b1[e]).then_inc(qs, 16)
            sy.dma_start(out=b2_sb[e % 2][:, :], in_=b2[e]).then_inc(qs, 16)
            for tc in range(NT):
                if e * NT + tc >= 2:
                    sy.wait_ge(hmm, 32 * (e * NT + tc - 1))
                for m in range(NM):
                    sy.dma_start(out=x1t[(e * NT + tc) % 2][:, m * TC:(m + 1) * TC],
                                 in_=x1b[m * 128:(m + 1) * 128, tc * TC:tc * TC + TC]).then_inc(qs, 16)

    @block.tensor
    def _(t):
        for e in range(NE_RUN):
            t.wait_ge(qg, 16 * max(QG_W1[e], QG_WEB[e]))
            for tc in range(NT):
                if e * NT + tc >= 2:
                    t.wait_ge(wbe, e * NT + tc - 1)
                t.matmul(pW[(e * NT + tc) % 2][:, :], onr_sb[0:1, 0:128],
                         web[0:1, tc * TC:tc * TC + TC], start=True, stop=True).then_inc(wbm, 1)
            for tc in range(NT):
                gi = e * NT + tc
                t.wait_ge(qs, 16 * (QS0 + SE * e + 2 + NM * (tc + 1)))
                for f in range(32):
                    fi = gi * 32 + f
                    if fi >= 2:
                        t.wait_ge(hv, fi - 1)
                    for k in range(NM):
                        ins = t.matmul(pH[fi % 2][:, :],
                                       w1_sb[:, k * FF + 128 * f:k * FF + 128 * f + 128],
                                       x1t[gi % 2][:, k * TC:(k + 1) * TC],
                                       start=(k == 0), stop=(k == NM - 1))
                    ins.then_inc(hmm, 1)
                t.wait_ge(hv, gi * 32 + 32)
                for dm in range(NM):
                    slot = gi * NM + dm
                    if slot >= 2:
                        t.wait_ge(yae, slot - 1)
                    for k in range(32):
                        ins = t.matmul(pY[slot % 2][:, :],
                                       w2_sb[:, k * DM + 128 * dm:k * DM + 128 * dm + 128],
                                       hsb[:, k * TC:(k + 1) * TC],
                                       start=(k == 0), stop=(k == 31))
                    ins.then_inc(ymm, 1)

    @block.scalar
    def _(s):
        for e in range(NE_RUN):
            s.wait_ge(ydv, NM * NT * e)
            s.wait_ge(qs, 16 * (QS0 + SE * e + 2))
            for tc in range(NT):
                gi = e * NT + tc
                s.wait_ge(wbm, gi + 1)
                s.activation(wbc[:, tc * TC:tc * TC + TC], pW[gi % 2][:, :],
                             AF.Identity).then_inc(wbe, 1)
            for tc in range(NT):
                gi = e * NT + tc
                s.wait_ge(ymm, NM * gi)
                for f in range(32):
                    fi = gi * 32 + f
                    s.wait_ge(hmm, fi + 1)
                    s.activation(hsb[:, f * TC:(f + 1) * TC], pH[fi % 2][:, :],
                                 AF.Gelu, bias=b1_sb[e % 2][:, f:f + 1]).then_inc(hv, 1)
                for dm in range(NM):
                    slot = gi * NM + dm
                    s.wait_ge(ymm, slot + 1)
                    if slot >= 2:
                        s.wait_ge(ydv, slot - 1)
                    s.activation(yt[slot % 2][:, :], pY[slot % 2][:, :], AF.Identity,
                                 bias=b2_sb[e % 2][:, dm:dm + 1]).then_inc(yae, 1)

    @block.vector
    def _(v):
        for e in range(NE_RUN):
            for tc in range(NT):
                gi = e * NT + tc
                for dm in range(NM):
                    slot = gi * NM + dm
                    v.wait_ge(yae, slot + 1)
                    if slot >= 4:
                        v.wait_ge(qg, 16 * QG_ACC[slot - 4])
                    v.tensor_mul(ysc[slot % 4][:, :], yt[slot % 2][:, :],
                                 wbc[:, tc * TC:tc * TC + TC]).then_inc(ydv, 1)

    estk.close()
    _exp_scope.__exit__(None, None, None)


    @block.sync
    def _(sy):
        sy.wait_ge(qg, 16 * QG_TOT)
        sy.wait_ge(qs, 16 * QS0)

    emit_ln_sum(ctx, block, "p4ln", x1f, facc, x2f, x2b)





def emit_p5(ctx, block):
    T, TC, NT = ctx.T, ctx.TC, ctx.NT
    NKC = T // 128
    catT = ctx.inp("catT"); cab = ctx.inp("cab_c")
    q2lT = ctx.inp("q2lT_r"); q2lb = ctx.inp("q2lb_r")
    kv2lT = ctx.inp("kv2lT_r"); kv2lb = ctx.inp("kv2lb_r")
    wqkT = ctx.inp("wqkT_r"); wqkb = ctx.inp("wqkb_r")
    cosr = ctx.inp("cos_r"); sinr = ctx.inp("sin_r"); rT = ctx.inp("rT_r")
    ident = ctx.inp("ident")
    x2b = ctx.park("x2b")
    qkvp = ctx.park("qkvp")
    qfkp = ctx.park("qfkp"); krotp = ctx.park("krotp"); vaugp = ctx.park("vaugp")

    qs = ctx.sem("p5_qs"); qg = ctx.sem("p5_qg")

    astk = ExitStack()
    cat_sb = astk.enter_context(ctx.nc.sbuf_tensor("p5_cat", [128, NM * 3072], BF16))
    x2t = astk.enter_context(ctx.nc.sbuf_tensor("p5_x2t", [128, NM * T], BF16))
    cab_sb = astk.enter_context(ctx.nc.sbuf_tensor("p5_cab", [128, 24], F32))
    stg = [astk.enter_context(ctx.nc.sbuf_tensor(f"p5_st{i}", [128, TC], BF16)) for i in range(4)]
    pQ = [astk.enter_context(ctx.nc.psum_tensor(f"p5_pQ{i}", [128, TC], F32)) for i in range(2)]
    amm = ctx.sem("p5_amm"); aev = ctx.sem("p5_aev")

    @block.sync
    def _(sy):
        sy.dma_start(out=cab_sb[:, :], in_=cab).then_inc(qs, 16)

    @block.gpsimd
    def _(g):
        for k in range(NM):
            g.dma_start(out=cat_sb[:, k * 3072:(k + 1) * 3072],
                        in_=catT[k * 128:(k + 1) * 128, :]).then_inc(qg, 16)
        for k in range(NM):
            g.dma_start(out=x2t[:, k * T:(k + 1) * T],
                        in_=x2b[k * 128:(k + 1) * 128, :]).then_inc(qg, 16)

    @block.tensor
    def _(t):
        t.wait_ge(qg, 16 * 2 * NM)
        for m in range(24):
            for tc in range(NT):
                gi = m * NT + tc
                if gi >= 2:
                    t.wait_ge(aev, gi - 1)
                for k in range(NM):
                    ins = t.matmul(pQ[gi % 2][:, :],
                                   cat_sb[:, k * 3072 + 128 * m:k * 3072 + 128 * m + 128],
                                   x2t[:, k * T + tc * TC:k * T + tc * TC + TC],
                                   start=(k == 0), stop=(k == NM - 1))
                ins.then_inc(amm, 1)

    @block.scalar
    def _(s):
        s.wait_ge(qs, 16)
        for m in range(24):
            for tc in range(NT):
                gi = m * NT + tc
                s.wait_ge(amm, gi + 1)
                if gi >= 4:
                    s.wait_ge(qs, 16 * (1 + gi - 3))
                s.activation(stg[gi % 4][:, :], pQ[gi % 2][:, :], AF.Identity,
                             bias=cab_sb[:, m:m + 1]).then_inc(aev, 1)

    @block.sync
    def _(sy):
        for m in range(24):
            for tc in range(NT):
                gi = m * NT + tc
                sy.wait_ge(aev, gi + 1)
                sy.dma_start(out=qkvp[m * 128:(m + 1) * 128, tc * TC:tc * TC + TC],
                             in_=stg[gi % 4][:, :]).then_inc(qs, 16)

    astk.close()
    QSA = 1 + 24 * NT

    bstk = ExitStack()
    sbb = lambda n, sh, dt: bstk.enter_context(ctx.nc.sbuf_tensor(n, sh, dt))
    psb = lambda n, sh, dt=F32: bstk.enter_context(ctx.nc.psum_tensor(n, sh, dt))
    q2l_sb = sbb("p5_q2l", [64, 32], BF16)
    kv2l_sb = sbb("p5_kv2l", [64, 32], BF16)
    wqk_sb = sbb("p5_wqk", [32, 32], BF16)
    q2lb_sb = sbb("p5_q2lb", [32, 1], F32)
    kv2lb_sb = sbb("p5_kv2lb", [32, 1], F32)
    wqkb_sb = sbb("p5_wqkb", [32, 1], F32)
    cos_sb = sbb("p5_cos", [32, T], BF16)
    sin_sb = sbb("p5_sin", [32, T], BF16)
    rT_sb = sbb("p5_rT", [32, 32], BF16)
    id_sb = sbb("p5_id", [32, 32], BF16)
    qh = [sbb(f"p5_qh{i}", [64, T], BF16) for i in range(2)]
    kh = [sbb(f"p5_kh{i}", [64, T], BF16) for i in range(2)]
    vh = [sbb(f"p5_vh{i}", [64, T], BF16) for i in range(2)]
    qlat = sbb("p5_qlat", [32, T], BF16)
    klat = sbb("p5_klat", [32, T], BF16)
    vlat = sbb("p5_vlat", [32, T], BF16)
    rota = sbb("p5_rota", [32, T], F32)
    rotb = sbb("p5_rotb", [32, T], BF16)
    qrot = sbb("p5_qrot", [32, T], BF16)
    krot = [sbb(f"p5_krot{i}", [32, T], BF16) for i in range(2)]
    qfk = [sbb(f"p5_qfk{i}", [32, T], BF16) for i in range(2)]
    vaug = [sbb(f"p5_va{i}", [128, 33 * NKC], BF16) for i in range(2)]
    pA = [psb(f"p5_pA{i}", [32, TC]) for i in range(3)]
    pB = [psb(f"p5_pB{i}", [32, TC]) for i in range(2)]
    pV = [psb(f"p5_pV{i}", [128, 32], BF16) for i in range(2)]
    pVd = psb("p5_pVd", [128, 32], BF16)

    am = ctx.sem("p5_am"); av = ctx.sem("p5_av")
    bm = ctx.sem("p5_bm"); bv = ctx.sem("p5_bv")
    tm = ctx.sem("p5_tm"); tv = ctx.sem("p5_tv")

    @block.gpsimd
    def _(g):
        g.wait_ge(qs, 16 * QSA)
        for ap, tile in ((q2lT, q2l_sb), (kv2lT, kv2l_sb), (cosr, cos_sb),
                         (sinr, sin_sb)):
            g.dma_start(out=tile[:, :], in_=ap[0:tile.shape[0], :]).then_inc(qg, 16)
        g.dma_start(out=rT_sb[:, :], in_=rT[0:32, :]).then_inc(qg, 16)
        g.dma_start(out=wqk_sb[:, :], in_=wqkT[0:32, :]).then_inc(qg, 16)
        g.dma_start(out=id_sb[:, :], in_=ident[0:32, 0:32]).then_inc(qg, 16)
        for h in range(NH):
            if h < 2:
                g.wait_ge(qs, 16 * QSA)
            else:
                g.wait_ge(am, 4 * NT * (h - 1))
            g.dma_start(out=qh[h % 2][:, :], in_=qkvp[64 * h:64 * h + 64, :]).then_inc(qg, 16)
            g.dma_start(out=kh[h % 2][:, :], in_=qkvp[1024 + 64 * h:1024 + 64 * h + 64, :]).then_inc(qg, 16)
            g.dma_start(out=vh[h % 2][:, :], in_=qkvp[2048 + 64 * h:2048 + 64 * h + 64, :]).then_inc(qg, 16)

    @block.sync
    def _(sy):
        for ap, tile in ((q2lb, q2lb_sb), (kv2lb, kv2lb_sb), (wqkb, wqkb_sb)):
            sy.dma_start(out=tile[:, :], in_=ap[0:32, :]).then_inc(qs, 16)

    QGB = 2 * NM + 7
    QSB = QSA + 3

    @block.tensor
    def _(t):
        t.wait_ge(qs, 16 * QSB)
        for h in range(NH):
            t.wait_ge(qg, 16 * (QGB + 3 * (h + 1)))
            for cls in range(3):
                src_t = (qh, kh, vh)[cls][h % 2]
                lT = q2l_sb if cls == 0 else kv2l_sb
                for tc in range(NT):
                    li = (4 * h + cls) * NT + tc
                    if li >= 3:
                        t.wait_ge(av, li - 2)
                    t.matmul(pA[li % 3][:, :], lT[:, :],
                             src_t[:, tc * TC:tc * TC + TC],
                             start=True, stop=True).then_inc(am, 1)
            for cls in range(2):
                lat = qlat if cls == 0 else klat
                for tc in range(NT):
                    bi = (2 * h + cls) * NT + tc
                    t.wait_ge(av, (4 * h + cls) * NT + tc + 1)
                    if bi >= 2:
                        t.wait_ge(bv, bi - 1)
                    t.matmul(pB[bi % 2][:, :], rT_sb[:, :],
                             lat[:, tc * TC:tc * TC + TC],
                             start=True, stop=True).then_inc(bm, 1)
            t.wait_ge(bv, (2 * h + 1) * NT)
            for tc in range(NT):
                li = (4 * h + 3) * NT + tc
                if li >= 3:
                    t.wait_ge(av, li - 2)
                t.matmul(pA[li % 3][:, :], wqk_sb[:, :],
                         qrot[:, tc * TC:tc * TC + TC],
                         start=True, stop=True).then_inc(am, 1)
            t.wait_ge(av, (4 * h + 3) * NT)
            for kc in range(NKC):
                vi = h * NKC + kc
                if vi >= 2:
                    t.wait_ge(tv, vi - 1)
                t.transpose(pV[vi % 2][:, :], vlat[:, kc * 128:kc * 128 + 128],
                            id_sb[:, :]).then_inc(tm, 1)
            t.transpose(pVd[:, :], vlat[:, 0:128], id_sb[:, :]).then_inc(tm, 1)

    @block.scalar
    def _(s):
        for h in range(NH):
            for cls in range(3):
                dst = (qlat, klat, vlat)[cls]
                bcol = q2lb_sb if cls == 0 else kv2lb_sb
                if cls == 0:
                    s.wait_ge(bm, (2 * h - 1) * NT if h >= 1 else 0)
                if cls == 1:
                    s.wait_ge(bm, 2 * h * NT)
                if cls == 2:
                    s.wait_ge(tm, (NKC + 1) * h)
                for tc in range(NT):
                    li = (4 * h + cls) * NT + tc
                    s.wait_ge(am, li + 1)
                    s.activation(dst[:, tc * TC:tc * TC + TC], pA[li % 3][:, :],
                                 AF.Identity, bias=bcol[:, 0:1]).then_inc(av, 1)
            if h >= 2:
                s.wait_ge(qs, 16 * (QSB + 3 * (h - 1)))
            for tc in range(NT):
                li = (4 * h + 3) * NT + tc
                s.wait_ge(am, li + 1)
                s.activation(qfk[h % 2][:, tc * TC:tc * TC + TC], pA[li % 3][:, :],
                             AF.Identity, bias=wqkb_sb[:, 0:1]).then_inc(av, 1)

    @block.vector
    def _(v):
        for h in range(NH):
            v.wait_ge(am, 4 * NT * h)
            for tc in range(NT):
                bi = (2 * h) * NT + tc
                sl = slice(tc * TC, tc * TC + TC)
                v.wait_ge(bm, bi + 1)
                v.wait_ge(av, (4 * h) * NT + tc + 1)
                v.tensor_mul(rota[:, sl], pB[bi % 2][:, :], sin_sb[:, sl])
                v.tensor_mul(rotb[:, sl], qlat[:, sl], cos_sb[:, sl])
                v.tensor_add(qrot[:, sl], rota[:, sl], rotb[:, sl]).then_inc(bv, 1)
            if h >= 2:
                v.wait_ge(qs, 16 * (QSB + 3 * (h - 1)))
            for tc in range(NT):
                bi = (2 * h + 1) * NT + tc
                sl = slice(tc * TC, tc * TC + TC)
                v.wait_ge(bm, bi + 1)
                v.wait_ge(av, (4 * h + 1) * NT + tc + 1)
                v.tensor_mul(rota[:, sl], pB[bi % 2][:, :], sin_sb[:, sl])
                v.tensor_mul(rotb[:, sl], klat[:, sl], cos_sb[:, sl])
                v.tensor_add(krot[h % 2][:, sl], rota[:, sl], rotb[:, sl]).then_inc(bv, 1)
            if h >= 2:
                v.wait_ge(qs, 16 * (QSB + 3 * (h - 1)))
            v.memset(vaug[h % 2][:, :], 1.0)
            for kc in range(NKC):
                vi = h * NKC + kc
                v.wait_ge(tm, h * (NKC + 1) + kc + 2)
                v.tensor_copy(vaug[h % 2][:, kc * 33:kc * 33 + 32],
                              pV[vi % 2][:, :]).then_inc(tv, 1)

    @block.sync
    def _(sy):
        for h in range(NH):
            gI = h // 4; b = h % 4
            sy.wait_ge(av, 4 * NT * (h + 1))
            sy.dma_start(out=qfkp[gI * 128 + 32 * b:gI * 128 + 32 * b + 32, :],
                         in_=qfk[h % 2][:, :]).then_inc(qs, 16)
            sy.wait_ge(bv, (2 * h + 2) * NT)
            sy.dma_start(out=krotp[gI * 128 + 32 * b:gI * 128 + 32 * b + 32, :],
                         in_=krot[h % 2][:, :]).then_inc(qs, 16)
            sy.wait_ge(tv, NKC * (h + 1))
            sy.dma_start(out=vaugp[:, h * 33 * NKC:(h + 1) * 33 * NKC],
                         in_=vaug[h % 2][:, :]).then_inc(qs, 16)
        sy.wait_ge(qs, 16 * (QSB + 3 * NH))

    if DBG_GATE:
        dbg5 = {}
        for n, tile in (("qlat", qlat), ("klat", klat), ("vlat", vlat),
                        ("qrot", qrot), ("krot5", krot[1]), ("qfk5", qfk[1])):
            dbg5[n] = (ctx.nc.dram_tensor("dbg5_" + n, list(tile.shape), BF16,
                                          kind="ExternalOutput").ap(), tile)

        @block.sync
        def _(sy):
            sy.wait_ge(ctx.sem("p5_tv"), NKC * NH)
            for n, (ap, tile) in dbg5.items():
                sy.dma_start(out=ap, in_=tile[:, :]).then_inc(ctx.sem("p5_qdbg"), 16)
            sy.wait_ge(ctx.sem("p5_qdbg"), 16 * len(dbg5))


    bstk.close()



def emit_p6(ctx, block):
    T, TC, NT = ctx.T, ctx.TC, ctx.NT
    NKC = T // 128
    NR = TC // 128
    mask = ctx.inp("mask01"); onr = ctx.inp("ones_row")
    krotp = ctx.park("krotp"); qfkp = ctx.park("qfkp"); vaugp = ctx.park("vaugp")
    cphp = ctx.park("cphp"); cbias = ctx.park("cbias")
    x2f = ctx.park("x2f"); a1p = ctx.park("a1p")
    x3f = ctx.park("x3f"); x3b = ctx.park("x3b")

    qs = ctx.sem("p6_qs"); qg = ctx.sem("p6_qg")

    mstk = ExitStack()
    sbm = lambda n, sh, dt: mstk.enter_context(ctx.nc.sbuf_tensor(n, sh, dt))
    psm = lambda n, sh, dt=F32: mstk.enter_context(ctx.nc.psum_tensor(n, sh, dt))

    kro = sbm("p6_kro", [128, 4 * T], BF16)
    vau = sbm("p6_vau", [128, NH * 33 * NKC], BF16)
    cph = sbm("p6_cph", [128, 4 * 1024], BF16)
    cb_sb = sbm("p6_cb", [128, 8], F32)
    msk = sbm("p6_msk", [128, NR * TC], BF16)
    onr_sb = sbm("p6_onr", [1, 512], BF16)
    qf = [sbm(f"p6_qf{i}", [128, 4 * TC], BF16) for i in range(2)]
    att = [sbm(f"p6_att{i}", [128, TC], BF16) for i in range(4)]
    ysb = sbm("p6_ysb", [32, TC], F32)
    rr0 = sbm("p6_rr0", [1, TC], F32)
    rrb = sbm("p6_rrb", [1, TC], BF16)
    ynp = [sbm(f"p6_ynp{i}", [128, TC], BF16) for i in range(4)]
    a1stg = [sbm(f"p6_a1s{i}", [128, TC], F32) for i in range(4)]
    pS = [psm(f"p6_pS{i}", [128, TC]) for i in range(4)]
    pY = psm("p6_pY", [33, TC])
    pBC = psm("p6_pBC", [32, TC])
    pC = [psm(f"p6_pC{i}", [128, TC]) for i in range(2)]

    qmm = ctx.sem("p6_qmm"); aex = ctx.sem("p6_aex"); dmk = ctx.sem("p6_dmk")
    avm = ctx.sem("p6_avm"); yev = ctx.sem("p6_yev"); rbv = ctx.sem("p6_rbv")
    bcm = ctx.sem("p6_bcm"); ydn = ctx.sem("p6_ydn")
    cpm = ctx.sem("p6_cpm"); cv6 = ctx.sem("p6_cv6"); a1n = ctx.sem("p6_a1n")

    def kcmax(qc):
        return (qc + 1) * NR
    CT = [0]
    for qc in range(NT):
        CT.append(CT[-1] + NH * kcmax(qc))
    def sidx(qc, h, kc):
        return CT[qc] + h * kcmax(qc) + kc
    def is_diag(qc, kc):
        return kc >= qc * NR
    DIAG = []
    dcnt = 0
    for qc in range(NT):
        for h in range(NH):
            for kc in range(kcmax(qc)):
                if is_diag(qc, kc):
                    dcnt += 1
                DIAG.append(dcnt)

    @block.gpsimd
    def _(g):
        for gI in range(4):
            g.dma_start(out=kro[:, gI * T:(gI + 1) * T],
                        in_=krotp[gI * 128:(gI + 1) * 128, :]).then_inc(qg, 16)
            g.dma_start(out=cph[:, gI * 1024:(gI + 1) * 1024],
                        in_=cphp[gI * 128:(gI + 1) * 128, :]).then_inc(qg, 16)
        g.dma_start(out=vau[:, :], in_=vaugp).then_inc(qg, 16)
        g.dma_start(out=msk[:, :], in_=mask).then_inc(qg, 16)
        g.dma_start(out=onr_sb[:, :], in_=onr).then_inc(qg, 16)

    STORE_CNT = []
    _c = 1
    for qc in range(NT):
        _c += 4
        if qc >= 1:
            for m in range(NM):
                _c += 1
                STORE_CNT.append(_c)
    for m in range(NM):
        _c += 1
        STORE_CNT.append(_c)
    QS_TOT6 = _c

    @block.sync
    def _(sy):
        sy.dma_start(out=cb_sb[:, :], in_=cbias).then_inc(qs, 16)
        for qc in range(NT):
            if qc >= 2:
                sy.wait_ge(qmm, CT[qc - 1])
            for gI in range(4):
                sy.dma_start(out=qf[qc % 2][:, gI * TC:(gI + 1) * TC],
                             in_=qfkp[gI * 128:(gI + 1) * 128, qc * TC:qc * TC + TC]).then_inc(qs, 16)
            if qc >= 1 and P6_TAIL:
                for m in range(NM):
                    idx = (qc - 1) * NM + m
                    sy.wait_ge(a1n, idx + 1)
                    sy.dma_start(out=a1p[m * 128:(m + 1) * 128, (qc - 1) * TC:(qc - 1) * TC + TC],
                                 in_=a1stg[idx % 4][:, :]).then_inc(qs, 16)
        if P6_TAIL:
            for m in range(NM):
                idx = (NT - 1) * NM + m
                sy.wait_ge(a1n, idx + 1)
                sy.dma_start(out=a1p[m * 128:(m + 1) * 128, (NT - 1) * TC:(NT - 1) * TC + TC],
                             in_=a1stg[idx % 4][:, :]).then_inc(qs, 16)
            sy.wait_ge(qs, 16 * QS_TOT6)
        else:
            sy.wait_ge(ctx.sem("p6_ydn"), P6_NH * NT)
            for gg in range(4):
                ap = ctx.nc.dram_tensor(f"dbg6_ynp{gg}", [128, TC], BF16,
                                        kind="ExternalOutput").ap()
                sy.dma_start(out=ap, in_=ynp[gg][:, :]).then_inc(qs, 16)
            sy.wait_ge(qs, 16 * (1 + 4 * NT + 4))

    @block.tensor
    def _(t):
        t.wait_ge(qg, 16 * 11)
        t.wait_ge(qs, 16)
        for qc in range(NT):
            t.wait_ge(qs, 16 * (1 + 4 * (qc + 1)))
            for h in range(P6_NH):
                gI = h // 4; b = h % 4
                km = kcmax(qc)
                for kc in range(km + 2):
                    if kc < km:
                        si = sidx(qc, h, kc)
                        if si >= 4:
                            t.wait_ge(aex, si - 3)
                        t.matmul(pS[si % 4][:, :],
                                 kro[32 * b:32 * b + 32, gI * T + kc * 128:gI * T + kc * 128 + 128],
                                 qf[qc % 2][32 * b:32 * b + 32, gI * TC:(gI + 1) * TC],
                                 start=True, stop=True,
                                 tile_position=(32 * b, 0)).then_inc(qmm, 1)
                    if kc == 1 and h >= 1:
                        t.wait_ge(rbv, qc * NH + h)
                        t.matmul(pBC[:, :], onr_sb[0:1, 0:32], rrb[:, :],
                                 start=True, stop=True).then_inc(bcm, 1)
                    if kc >= 2:
                        kd = kc - 2
                        sd = sidx(qc, h, kd)
                        if kd == 0:
                            t.wait_ge(rbv, qc * NH + h)
                        t.wait_ge(dmk, DIAG[sd])
                        t.wait_ge(aex, sd + 1)
                        t.matmul(pY[:, :], vau[:, (h * NKC + kd) * 33:(h * NKC + kd) * 33 + 33],
                                 att[sd % 4][:, :],
                                 start=(kd == 0), stop=(kd == km - 1)).then_inc(avm, 1)
            t.wait_ge(rbv, qc * NH + P6_NH)
            t.matmul(pBC[:, :], onr_sb[0:1, 0:32], rrb[:, :],
                     start=True, stop=True).then_inc(bcm, 1)
            if not P6_TAIL:
                continue
            t.wait_ge(ydn, NH * (qc + 1))
            for m in range(NM):
                ci = qc * NM + m
                if ci >= 2:
                    t.wait_ge(a1n, ci - 1)
                for gI in range(4):
                    ins = t.matmul(pC[ci % 2][:, :],
                                   cph[:, gI * 1024 + 128 * m:gI * 1024 + 128 * m + 128],
                                   ynp[gI][:, :],
                                   start=(gI == 0), stop=(gI == 3))
                ins.then_inc(cpm, 1)

    @block.scalar
    def _(s):
        isc = 1.0 / float(np.sqrt(LAT))
        for qc in range(NT):
            for h in range(P6_NH):
                for kc in range(kcmax(qc)):
                    si = sidx(qc, h, kc)
                    s.wait_ge(qmm, si + 1)
                    if si >= 4:
                        s.wait_ge(avm, si - 3)
                    s.activation(att[si % 4][:, :], pS[si % 4][:, :], AF.Exp,
                                 scale=isc).then_inc(aex, 1)
                hi = qc * NH + h
                s.wait_ge(avm, sidx(qc, h, kcmax(qc) - 1) + 1)
                s.activation(ysb[:, :], pY[0:32, :], AF.Identity).then_inc(yev, 1)
            if not P6_TAIL:
                continue
            for m in range(NM):
                ci = qc * NM + m
                s.wait_ge(cpm, ci + 1)
                if ci >= 4:
                    s.wait_ge(qs, 16 * STORE_CNT[ci - 4])
                s.activation(a1stg[ci % 4][:, :], pC[ci % 2][:, :], AF.Identity,
                             bias=cb_sb[:, m:m + 1]).then_inc(a1n, 1)

    @block.vector
    def _(v):
        for qc in range(NT):
            for h in range(P6_NH):
                gI = h // 4; b = h % 4
                for kc in range(kcmax(qc)):
                    si = sidx(qc, h, kc)
                    if is_diag(qc, kc):
                        r = kc - qc * NR
                        v.wait_ge(aex, si + 1)
                        v.tensor_mul(att[si % 4][:, :], att[si % 4][:, :],
                                     msk[:, r * TC:(r + 1) * TC]).then_inc(dmk, 1)
                hi = qc * NH + h
                v.wait_ge(yev, hi + 1)
                v.reciprocal(rr0[:, :], pY[32:33, :])
                v.tensor_copy(rrb[:, :], rr0[:, :]).then_inc(rbv, 1)
                v.wait_ge(bcm, hi + 1)
                if qc >= 1 and h < 4:
                    v.wait_ge(cpm, NM * qc)
                v.tensor_mul(ynp[gI][32 * b:32 * b + 32, :], ysb[:, :],
                             pBC[:, :]).then_inc(ydn, 1)

    mstk.close()

    if P6_TAIL:
        emit_ln_sum(ctx, block, "p6ln", x2f, a1p, x3f, x3b)



def emit_p7(ctx, block):
    T, TC, NT = ctx.T, ctx.TC, ctx.NT
    w0 = ctx.inp("l1w0"); w1a = ctx.inp("l1w1"); w2a = ctx.inp("l1w2")
    lb = ctx.inp("l1b_c")
    x3f = ctx.park("x3f"); x3b = ctx.park("x3b"); f1p = ctx.park("f1p")
    outf = ctx.out_park()

    qs = ctx.sem("p7_qs"); qg = ctx.sem("p7_qg")
    mstk = ExitStack()
    sbm = lambda n, sh, dt: mstk.enter_context(ctx.nc.sbuf_tensor(n, sh, dt))
    psm = lambda n, sh: mstk.enter_context(ctx.nc.psum_tensor(n, sh, F32))
    w_sb = [sbm(f"p7_w{i}", [128, NM * 1024], BF16) for i in range(3)]
    lb_sb = sbm("p7_lb", [128, 8], F32)
    x3t = [sbm(f"p7_x3{i}", [128, NM * TC], BF16) for i in range(2)]
    h1 = sbm("p7_h1", [128, NM * TC], BF16)
    h2 = sbm("p7_h2", [128, NM * TC], BF16)
    f1s = [sbm(f"p7_f1{i}", [128, TC], F32) for i in range(4)]
    pP = [psm(f"p7_p{i}", [128, TC]) for i in range(2)]

    mm = ctx.sem("p7_mm"); ev = ctx.sem("p7_ev"); fn = ctx.sem("p7_fn")

    @block.gpsimd
    def _(g):
        for i, w in enumerate((w0, w1a, w2a)):
            for k in range(NM):
                g.dma_start(out=w_sb[i][:, k * 1024:(k + 1) * 1024],
                            in_=w[k * 128:(k + 1) * 128, :]).then_inc(qg, 16)

    STORE_CNT7 = []
    _c = 1
    for tc in range(NT):
        _c += NM
        if tc >= 1:
            for m in range(NM):
                _c += 1
                STORE_CNT7.append(_c)
    for m in range(NM):
        _c += 1
        STORE_CNT7.append(_c)
    QS_TOT7 = _c

    @block.sync
    def _(sy):
        sy.dma_start(out=lb_sb[:, :], in_=lb).then_inc(qs, 16)
        for tc in range(NT):
            if tc >= 2:
                sy.wait_ge(mm, 3 * NM * (tc - 1))
            for m in range(NM):
                sy.dma_start(out=x3t[tc % 2][:, m * TC:(m + 1) * TC],
                             in_=x3b[m * 128:(m + 1) * 128, tc * TC:tc * TC + TC]).then_inc(qs, 16)
            if tc >= 1:
                for m in range(NM):
                    idx = (tc - 1) * NM + m
                    sy.wait_ge(fn, idx + 1)
                    sy.dma_start(out=f1p[m * 128:(m + 1) * 128, (tc - 1) * TC:(tc - 1) * TC + TC],
                                 in_=f1s[idx % 4][:, :]).then_inc(qs, 16)
        for m in range(NM):
            idx = (NT - 1) * NM + m
            sy.wait_ge(fn, idx + 1)
            sy.dma_start(out=f1p[m * 128:(m + 1) * 128, (NT - 1) * TC:(NT - 1) * TC + TC],
                         in_=f1s[idx % 4][:, :]).then_inc(qs, 16)
        sy.wait_ge(qs, 16 * QS_TOT7)

    @block.tensor
    def _(t):
        t.wait_ge(qg, 16 * 3 * NM)
        for tc in range(NT):
            t.wait_ge(qs, 16 * (1 + NM * (tc + 1)))
            for stage in range(3):
                src = (x3t[tc % 2], h1, h2)[stage]
                for m in range(NM):
                    gi = (tc * 3 + stage) * NM + m
                    if gi >= 2:
                        prev = gi - 2
                        tp = prev // (3 * NM); sp = (prev // NM) % 3; mp = prev % NM
                        if sp == 2:
                            t.wait_ge(fn, NM * tp + mp + 1)
                        else:
                            t.wait_ge(ev, 2 * NM * tp + NM * sp + mp + 1)
                    if stage > 0:
                        t.wait_ge(ev, 2 * NM * tc + NM * stage)
                    for k in range(NM):
                        ins = t.matmul(pP[gi % 2][:, :],
                                       w_sb[stage][:, k * 1024 + 128 * m:k * 1024 + 128 * m + 128],
                                       src[:, k * TC:(k + 1) * TC],
                                       start=(k == 0), stop=(k == NM - 1))
                    ins.then_inc(mm, 1)

    @block.scalar
    def _(s):
        s.wait_ge(qs, 16)
        for tc in range(NT):
            for stage in range(3):
                dst = (h1, h2, None)[stage]
                for m in range(NM):
                    gi = (tc * 3 + stage) * NM + m
                    s.wait_ge(mm, gi + 1)
                    if stage < 2:
                        if tc >= 1:
                            s.wait_ge(mm, ((tc - 1) * 3 + stage + 1) * NM + NM)
                        s.activation(dst[:, m * TC:(m + 1) * TC], pP[gi % 2][:, :],
                                     AF.Identity).then_inc(ev, 1)
                    else:
                        idx = tc * NM + m
                        if idx >= 4:
                            s.wait_ge(qs, 16 * STORE_CNT7[idx - 4])
                        s.activation(f1s[idx % 4][:, :], pP[gi % 2][:, :], AF.Tanh,
                                     bias=lb_sb[:, m:m + 1]).then_inc(fn, 1)

    mstk.close()

    emit_ln_sum(ctx, block, "p7ln", x3f, f1p, outf, None)



EMITTERS = {"p0": emit_p0, "p1": emit_p1, "p2": emit_p2, "p3": emit_p3, "p4": emit_p4, "p5": emit_p5, "p6": emit_p6, "p7": emit_p7}


def build(T, phases, prep_shapes):
    nc = bass.Bass("TRN2", target_bir_lowering=False)
    stack = ExitStack()
    ctx = Ctx(nc, T, stack, phases, prep_shapes)
    with nc.Block() as block:
        for i, ph in enumerate(phases):
            ctx.begin_phase()
            with nc.named_scope(ph):
                EMITTERS[ph](ctx, block)
            ctx.end_phase()
            if i < len(phases) - 1:
                barrier(ctx, block, ph + "a")
                sems = ctx.phase_semlist
                ctx.phase_semlist = []
                keep = {k: v for k, v in ctx.sems.items()
                        if k == "bar" or "_q" in k}
                ctx.sems = keep
                if sems:
                    def clr(g, sems=sems):
                        for s in sems:
                            g.sem_clear(s)
                    block.gpsimd(clr)
                    barrier(ctx, block, ph + "b")
                    for s in sems:
                        nc.release_semaphore(s)
    stack.close()
    return nc, ctx


def make_prep_shapes(prep, T):
    shapes = {k: (v.shape, v.dtype.type if hasattr(v.dtype, 'type') else v.dtype)
              for k, v in prep.items()}
    shapes = {k: (v.shape, (BF if v.dtype == BF else np.float32)) for k, v in prep.items()}
    shapes["xf"] = ((DM, T), np.float32)
    return shapes






ALL_PHASES = ["p0", "p1", "p2", "p3", "p4", "p5", "p6", "p7"]


def kernel(**inputs):
    x = np.asarray(inputs["L0_attn_x"])
    T = int(x.shape[1])
    prep, xfs = host_prep(inputs, T)
    shapes = make_prep_shapes(prep, T)
    nc, ctx = build(T, ALL_PHASES, shapes)
    names = list(ctx.inputs.keys())
    in_maps = []
    for c in range(x.shape[0]):
        m = {}
        for n in names:
            m["i_" + n] = xfs[c] if n == "xf" else prep[n]
        in_maps.append(m)
    res = run_bass_kernel_spmd(nc, in_maps, core_ids=list(range(len(in_maps))))
    out = np.stack([np.ascontiguousarray(np.asarray(r["outf"]).T) for r in res.results])
    return out.astype(np.float32)
